# version 05
# baseline (speedup 1.0000x reference)
import numpy as np
import ml_dtypes
from contextlib import ExitStack
import concourse.bass as bass
import concourse.mybir as mybir
from concourse.alu_op_type import AluOpType as OP
from concourse.bass_utils import run_bass_kernel_spmd

F32 = mybir.dt.float32
BF16 = mybir.dt.bfloat16
I32 = mybir.dt.int32
AF = mybir.ActivationFunctionType
AX = mybir.AxisListType
ET = mybir.EngineType
RG = [[0, 1, 2, 3, 4, 5, 6, 7]]

NC, L, T, TH = 8, 16384, 2048, 2051
D, E, S, RK, H = 2048, 4096, 16, 128, 8192
NT_D, NT_E, NT_H = 16, 32, 64
EPS = 1e-5
SPCL = 30.0
TH_CH = [(0, 512), (512, 512), (1024, 512), (1536, 512), (2048, 3)]
T_CH = [(0, 512), (512, 512), (1024, 512), (1536, 512)]
ARENA = 48128


class Plan:
    ENGINES = ("sync", "gpsimd", "vector", "scalar", "tensor")

    def __init__(self, nc, st):
        self.nc, self.st = nc, st
        self.q = {e: [] for e in self.ENGINES}
        self.sems, self.cnt = {}, {}

    def sem(self, name):
        if name not in self.sems:
            self.sems[name] = self.st.enter_context(self.nc.semaphore(name))
            self.cnt[name] = 0
        return self.sems[name]

    def now(self, name):
        self.sem(name)
        return self.cnt[name]

    def step(self, eng, fn, waits=(), inc=None, inc_k=1):
        waits = [(self.sem(n), v) for (n, v) in waits if v > 0]
        sem = self.sem(inc) if inc else None
        if inc:
            self.cnt[inc] += inc_k

        def run(engine, waits=waits, fn=fn, sem=sem, inc_k=inc_k):
            for s, v in waits:
                engine.wait_ge(s, v)
            instr = fn(engine)
            if sem is not None:
                instr.then_inc(sem, inc_k)
        self.q[eng].append(run)
        return self.cnt[inc] if inc else None

    def dma(self, eng, out, in_, waits=(), track="dS", selfwait=True, **kw):
        if selfwait:
            waits = list(waits) + [(track, self.now(track))]
        v = self.step(eng, lambda e: e.dma_start(out=out, in_=in_, **kw),
                      waits=waits, inc=track, inc_k=16)
        return (track, v)

    def dma_dyn(self, eng, out, in_fn, waits=(), track="dS", selfwait=True,
                **kw):
        if selfwait:
            waits = list(waits) + [(track, self.now(track))]
        v = self.step(eng, lambda e: e.dma_start(out=out, in_=in_fn(), **kw),
                      waits=waits, inc=track, inc_k=16)
        return (track, v)

    def fence(self):
        vals = [(n, self.cnt[n]) for n in self.cnt]
        sems = self.sems
        for name in self.ENGINES:
            def run(engine, vals=vals):
                for n, v in vals:
                    if v > 0:
                        engine.wait_ge(sems[n], v)
            self.q[name].append(run)

    def emit(self, block):
        for name in self.ENGINES:
            steps = self.q[name]

            def body(engine, steps=steps):
                for s in steps:
                    s(engine)
            getattr(block, name)(body)


class Arena:
    def __init__(self, t):
        self.t, self.top, self.marks, self.peak = t, 0, [], 0

    def f32(self, cols):
        a = self.top
        self.top += cols
        self.peak = max(self.peak, self.top)
        assert self.top <= ARENA, f"arena overflow {self.top}"
        return self.t[:, a:a + cols]

    def bf16(self, n):
        assert n % 2 == 0
        a = self.top
        self.top += n // 2
        self.peak = max(self.peak, self.top)
        assert self.top <= ARENA, f"arena overflow {self.top}"
        return self.t[:, a:a + n // 2].bitcast(BF16)

    def push(self):
        self.marks.append(self.top)

    def pop(self):
        self.top = self.marks.pop()


class _Stop(Exception):
    pass


def build(upto=9):
    nc = bass.Bass("TRN2", target_bir_lowering=False, debug=False)
    dram = {}

    def din(name, shape, dt=F32):
        dram[name] = nc.dram_tensor(name, shape, dt, kind="ExternalInput").ap()
        return dram[name]

    def dint(name, shape, dt=F32):
        dram[name] = nc.dram_tensor(name, shape, dt, kind="Internal").ap()
        return dram[name]

    xfm = din("xfm", [D, TH])
    w_ip = din("w_ip", [NT_H, D, 128])
    w_xp = din("w_xp", [2, E, 128])
    w_dts = din("w_dts", [128, 512])
    w_op = din("w_op", [NT_D, E, 128])
    w_fp = din("w_fp", [NT_H, D, 128])
    w_fo = din("w_fo", [NT_D, E, 128])
    w_fnt = din("w_fnt", [NT_D, D, 128])
    w_fnr = din("w_fnr", [D, D])
    cvw = din("cvw", [E, 4])
    for nm in ("cvb", "dsk"):
        din(nm, [128, NT_E])
    din("dtbs", [128, 4])
    din("alog", [128, 4 * S])
    for nm in ("ln1w", "ln1b", "ln2w", "ln2b", "fnb"):
        din(nm, [128, NT_D])
    din("ident", [128, 128])
    din("identu", [128, 128], mybir.dt.uint8)
    din("onecol", [128, 1], BF16)
    din("onefull", [128, 128], BF16)
    din("m3", [128, 1])
    din("zcol", [128, 64])
    din("zb", [128, 128], BF16)
    din("cvals", [128, 4])
    din("coff", [1, 1], I32)
    din("coff2", [1, 1], I32)

    out = nc.dram_tensor("out", [D, T], F32, kind="ExternalOutput").ap()
    dbg = nc.dram_tensor("dbg", [256, T], F32, kind="ExternalOutput").ap()

    EP = E + 128
    sz_sp = dint("sz_sp", [E, T])
    xcf_sp = dint("xcf_sp", [E, T])
    y_sp = dint("y_sp", [E, T], BF16)
    hg_sp = dint("hg_sp", [E, T], BF16)
    t_sp = dint("t_sp", [D, T])
    x1_sp = dint("x1_sp", [D, T])
    ab_sp = dint("ab_sp", [D, T], BF16)
    o_sp = dint("o_sp", [D, T], BF16)
    agi = dint("agi", [EP, T], BF16)
    agoq = dint("agoq", [4 * NC * 1024, T], BF16)
    agod = dint("agod", [NC * 128, T], BF16)
    bci = dint("bci", [2 * S, T], BF16)
    bco = dint("bco", [NC * 2 * S, T], BF16)
    ygi = dint("ygi", [E, T], BF16)
    ygo = dint("ygo", [NC * E, T], BF16)
    al_dr = dint("al_dr", [NT_D, 128])
    myag = dint("myag", [NC * 512, T], BF16)
    myy = dint("myy", [E, T], BF16)
    row_dr = dint("row_dr", [4, TH])
    st_dr = dint("st_dr", [2 * 128, TH])

    st = ExitStack()
    with st:
        P = Plan(nc, st)

        def sb(name, shape, dt):
            return st.enter_context(nc.sbuf_tensor(name, shape, dt))

        s_par = {}
        for nm, w in [("cvb", NT_E), ("dtbs", 4), ("dsk", NT_E), ("ln1w", NT_D),
                      ("ln1b", NT_D), ("ln2w", NT_D), ("ln2b", NT_D), ("fnb", NT_D)]:
            s_par[nm] = sb("s_" + nm, [128, w], F32)
        s_acol = sb("s_acol", [128, 4 * S], F32)
        s_ident = sb("s_ident", [128, 128], F32)
        s_identu = sb("s_identu", [128, 128], mybir.dt.uint8)
        s_one = sb("s_one", [128, 1], BF16)
        s_onef = sb("s_onef", [128, 128], BF16)
        s_m3 = sb("s_m3", [128, 1], F32)
        s_cvw = sb("s_cvw", [128, NT_E * 4], F32)
        s_hst = sb("s_hst", [128, 64], F32)
        s_coff = sb("s_coff", [1, 1], I32)
        s_coff2 = sb("s_coff2", [1, 1], I32)
        s_zb = sb("s_zb", [128, 128], BF16)
        s_cvals = sb("s_cvals", [128, 4], F32)
        s_dtws = sb("s_dtws", [128, 512], BF16)
        s_identb = sb("s_identb", [128, 128], BF16)

        s_alpha = sb("s_alpha", [128, NT_D], F32)
        s_arow = sb("s_arow", [1, D], F32)
        s_a16 = sb("s_a16", [NT_D, 128], F32)
        arena_t = sb("arena", [128, ARENA], F32)
        A = Arena(arena_t)
        hold = {}

        pb = [st.enter_context(nc.psum_tensor(f"pb{i}", [128, 512], F32))
              for i in range(8)]

        def stop_dump(a_ap=None, b_ap=None):
            w = [(n, P.cnt[n]) for n in P.cnt]
            if a_ap is not None:
                P.dma("gpsimd", dbg[0:a_ap.shape[0], 0:a_ap.shape[1]], a_ap,
                      waits=w, track="ddbg")
            if b_ap is not None:
                P.dma("gpsimd", dbg[128:128 + b_ap.shape[0], 0:b_ap.shape[1]],
                      b_ap, waits=w, track="ddbg")
            P.fence()
            raise _Stop

        def col(par, j):
            return s_par[par][:, j:j + 1]

        try:
            for nm in ("cvb", "dtbs", "dsk", "ln1w", "ln1b", "ln2w", "ln2b", "fnb"):
                P.dma("sync", s_par[nm][:], dram[nm], track="d0")
            P.dma("sync", s_ident[:], dram["ident"], track="d0")
            P.dma("sync", s_identu[:], dram["identu"], track="d0")
            P.dma("sync", s_m3[:], dram["m3"], track="d0")
            P.dma("sync", s_one[:], dram["onecol"], track="d0")
            P.dma("sync", s_onef[:], dram["onefull"], track="d0")
            P.dma("sync", s_hst[:], dram["zcol"], track="d0")
            P.dma("sync", s_coff[:], dram["coff"], track="d0")
            P.dma("sync", s_coff2[:], dram["coff2"], track="d0")
            P.dma("sync", s_zb[:], dram["zb"], track="d0")
            P.dma("sync", s_cvals[:], dram["cvals"], track="d0")
            nc.const_aps.register(F32, EPS, s_cvals[:, 0:1])
            nc.const_aps.register(F32, 1.0, s_cvals[:, 1:2])
            nc.const_aps.register(F32, 0.0, s_cvals[:, 2:3])
            P.dma("sync", s_cvw[:].rearrange("p (j k) -> p j k", k=4),
                  cvw.rearrange("(j p) k -> p j k", p=128), track="d0")
            P.dma("sync", s_acol[:], dram["alog"], track="d0")
            P.dma("gpsimd", s_dtws[:], dram["w_dts"], track="d0g")
            P.dma("gpsimd", s_identb[:], dram["ident"], track="d0g")
            vs = P.now("d0")
            va0 = P.step("scalar", lambda e: e.activation(
                out=s_acol[:], in_=s_acol[:], func=AF.Exp),
                waits=[("d0", vs)], inc="cA")
            P.step("scalar", lambda e: e.activation(
                out=s_acol[:], in_=s_acol[:], func=AF.Copy, scale=-1.0),
                waits=[("cA", va0)], inc="cA")
            vcoff = vs

            def _load_off(e):
                hold["off"] = nc.values_load(s_coff[:], engines=[ET.SP],
                                             min_val=0, max_val=3584,
                                             skip_runtime_bounds_check=True)
                hold["off2"] = nc.values_load(s_coff2[:], engines=[ET.SP],
                                              min_val=0, max_val=25088,
                                              skip_runtime_bounds_check=True)
                return None
            P.q["sync"].append(lambda e: (e.wait_ge(P.sems["d0"], vcoff),
                                          _load_off(e)) and None)

            A.push()
            wtmp = A.f32(2 * D)
            s_wt = [wtmp[:, 0:D], wtmp[:, D:2 * D]]
            redv = [0] * NT_D
            for j in range(NT_D):
                w = [("cV", redv[j - 2])] if j >= 2 else []
                v = P.dma("sync", s_wt[j % 2], w_fnr[j * 128:(j + 1) * 128, :],
                          waits=w, track=f"wA{j % 2}")
                redv[j] = P.step("vector", lambda e, j=j: e.tensor_reduce(
                    out=s_alpha[:, j:j + 1], in_=s_wt[j % 2],
                    axis=AX.X, op=OP.add, apply_absolute_value=True),
                    waits=[v], inc="cV")
            P.step("vector", lambda e: e.tensor_scalar(
                out=s_alpha[:], in0=s_alpha[:], scalar1=1.0 / D, scalar2=None,
                op0=OP.mult), waits=[("cV", redv[NT_D - 1])], inc="cV")
            P.step("tensor", lambda e: e.transpose(pb[0][0:NT_D, 0:128], s_alpha[:],
                                                   s_ident[:]),
                   waits=[("cV", P.now("cV")), ("d0", P.now("d0"))], inc="cT")
            P.step("scalar", lambda e: e.activation(out=s_a16[:], in_=pb[0][0:NT_D, 0:128],
                                                    func=AF.Copy),
                   waits=[("cT", P.now("cT"))], inc="cA")
            va16 = P.dma("gpsimd", al_dr, s_a16[:],
                         waits=[("cA", P.now("cA"))], track="dal")
            P.dma("gpsimd", s_arow[:],
                  al_dr.rearrange("a b -> (a b)").unsqueeze(0),
                  waits=[va16], track="dal2")
            A.pop()
            P.fence()

            def stats_resident(tile_of, ntiles, grid, sqb, sum_t, sq_t, ready):
                for (c0, cw) in grid:
                    sqmm = [0] * ntiles
                    for j in range(ntiles):
                        P.step("tensor", lambda e, j=j, c0=c0, cw=cw: e.matmul(
                            pb[6][:, 0:cw], s_onef[:], tile_of(j)[:, c0:c0 + cw],
                            start=(j == 0), stop=(j == ntiles - 1)),
                            waits=[("cA", ready[j])],
                            inc="cT" if j == ntiles - 1 else None)
                        vq = P.step("scalar", lambda e, j=j, c0=c0, cw=cw: e.activation(
                            out=sqb[j % 2][:, 0:cw], in_=tile_of(j)[:, c0:c0 + cw],
                            func=AF.Square),
                            waits=[("cT", sqmm[j - 2])] if j >= 2 else [], inc="cA")
                        sqmm[j] = P.step("tensor", lambda e, j=j, cw=cw: e.matmul(
                            pb[7][:, 0:cw], s_onef[:], sqb[j % 2][:, 0:cw],
                            start=(j == 0), stop=(j == ntiles - 1)),
                            waits=[("cA", vq)], inc="cT")
                    vt = P.now("cT")
                    P.step("scalar", lambda e, c0=c0, cw=cw: e.activation(
                        out=sum_t[:, c0:c0 + cw], in_=pb[6][:, 0:cw], func=AF.Copy),
                        waits=[("cT", vt)], inc="cA")
                    va = P.step("scalar", lambda e, c0=c0, cw=cw: e.activation(
                        out=sq_t[:, c0:c0 + cw], in_=pb[7][:, 0:cw], func=AF.Copy),
                        inc="cA")
                    P.q["tensor"].append(
                        lambda e, va=va: e.wait_ge(P.sems["cA"], va))
                return P.now("cA")

            def stats_streamed(load_fn, sqb, bufs, sum_t, sq_t):
                lastmm = [0] * NT_D
                for j in range(NT_D):
                    b = bufs[j % 2]
                    v = load_fn(j, b, [("cT", lastmm[j - 2])] if j >= 2 else [])
                    vq = P.step("scalar", lambda e, b=b, j=j: e.activation(
                        out=sqb[j % 2][:, 0:T], in_=b[:, 0:T], func=AF.Square),
                        waits=[v] + ([("cT", lastmm[j - 2])] if j >= 2 else []),
                        inc="cA")
                    for ci, (c0, cw) in enumerate(T_CH):
                        P.step("tensor", lambda e, b=b, c0=c0, cw=cw, ci=ci, j=j: e.matmul(
                            pb[ci][:, 0:cw], s_onef[:], b[:, c0:c0 + cw],
                            start=(j == 0), stop=(j == NT_D - 1)),
                            waits=[v] if ci == 0 else [],
                            inc=None)
                        lastmm[j] = P.step("tensor",
                            lambda e, c0=c0, cw=cw, ci=ci, j=j: e.matmul(
                            pb[4 + ci][:, 0:cw], s_onef[:], sqb[j % 2][:, c0:c0 + cw],
                            start=(j == 0), stop=(j == NT_D - 1)),
                            waits=[("cA", vq)] if ci == 0 else [],
                            inc="cT")
                vt = P.now("cT")
                for ci, (c0, cw) in enumerate(T_CH):
                    P.step("scalar", lambda e, c0=c0, cw=cw, ci=ci: e.activation(
                        out=sum_t[:, c0:c0 + cw], in_=pb[ci][:, 0:cw], func=AF.Copy),
                        waits=[("cT", vt)] if ci == 0 else [], inc="cA")
                    P.step("scalar", lambda e, c0=c0, cw=cw, ci=ci: e.activation(
                        out=sq_t[:, c0:c0 + cw], in_=pb[4 + ci][:, 0:cw],
                        func=AF.Copy), inc="cA")
                return P.now("cA")

            def ln_math(stats_done, sum_t, sq_t, m_t, i_t, W, denom,
                        chain=False):
                va = P.step("scalar", lambda e: e.activation(
                    out=m_t[:, 0:W], in_=sum_t[:, 0:W], func=AF.Copy,
                    scale=1.0 / denom), waits=[("cA", stats_done)], inc="cA")
                va = P.step("scalar", lambda e: e.activation(
                    out=sq_t[:, 0:W], in_=sq_t[:, 0:W], func=AF.Copy,
                    scale=1.0 / denom), waits=[("cA", va)], inc="cA")
                vv = P.step("vector", lambda e: e.tensor_tensor(
                    out=sum_t[:, 0:W], in0=m_t[:, 0:W], in1=m_t[:, 0:W],
                    op=OP.mult), waits=[("cA", va)], inc="cV")
                vv = P.step("vector", lambda e: e.tensor_tensor(
                    out=sq_t[:, 0:W], in0=sq_t[:, 0:W], in1=sum_t[:, 0:W],
                    op=OP.subtract), waits=[("cV", vv)], inc="cV")
                va = P.step("scalar", lambda e: e.activation(
                    out=sum_t[:, 0:W], in_=sq_t[:, 0:W], func=AF.Sqrt,
                    bias=EPS), waits=[("cV", vv)], inc="cA")
                vv = P.step("vector", lambda e: e.reciprocal(
                    out=i_t[:, 0:W], in_=sum_t[:, 0:W]),
                    waits=[("cA", va)], inc="cV")
                if not chain:
                    return vv
                vv = P.step("vector", lambda e: e.tensor_tensor(
                    out=sum_t[:, 0:W], in0=i_t[:, 0:W], in1=i_t[:, 0:W],
                    op=OP.mult), waits=[("cV", vv)], inc="cV")
                vv = P.step("vector", lambda e: e.tensor_tensor(
                    out=sum_t[:, 0:W], in0=sq_t[:, 0:W], in1=sum_t[:, 0:W],
                    op=OP.mult), waits=[("cV", vv)], inc="cV")
                va = P.step("scalar", lambda e: e.activation(
                    out=sum_t[:, 0:W], in_=sum_t[:, 0:W], func=AF.Sqrt,
                    bias=EPS), waits=[("cV", vv)], inc="cA")
                vv = P.step("vector", lambda e: e.reciprocal(
                    out=sum_t[:, 0:W], in_=sum_t[:, 0:W]),
                    waits=[("cA", va)], inc="cV")
                vv = P.step("vector", lambda e: e.tensor_copy(
                    sq_t[:, 0:W], i_t[:, 0:W]), waits=[("cV", vv)], inc="cV")
                return P.step("vector", lambda e: e.tensor_tensor(
                    out=i_t[:, 0:W], in0=i_t[:, 0:W], in1=sum_t[:, 0:W],
                    op=OP.mult), waits=[("cV", vv)], inc="cV")

            def slab_load(dst, wsrc, j, waits=(), track="dG"):
                return P.dma("gpsimd", dst.rearrange("p (kt m) -> p kt m", m=128),
                             wsrc[j].rearrange("(kt p) m -> p kt m", p=128),
                             waits=waits, track=track)

            A.push()
            xb_all = A.bf16(NT_D * (TH + 1))

            def xb_t(j):
                return xb_all[:, j * (TH + 1):j * (TH + 1) + TH]

            A.push()
            xf = [A.f32(TH), A.f32(TH)]
            sqb = [A.bf16(512), A.bf16(512)]
            castv = [0] * NT_D
            for j in range(NT_D):
                w = [("cA", castv[j - 2])] if j >= 2 else []
                v = P.dma("sync", xf[j % 2], xfm[j * 128:(j + 1) * 128, :],
                          waits=w, track=f"xf{j % 2}")
                castv[j] = P.step("scalar", lambda e, j=j: e.activation(
                    out=xb_t(j), in_=xf[j % 2], func=AF.Copy),
                    waits=[v], inc="cA")
            sum_t = xf[0]
            sq_t = xf[1]
            repm = A.f32(TH)
            repi = A.f32(TH)
            sdone = stats_resident(xb_t, NT_D, TH_CH, sqb, sum_t, sq_t, castv)
            vr2 = ln_math(sdone, sum_t, sq_t, repm, repi, TH, D, chain=True)
            vsp1 = P.dma("sync", st_dr[0:128, :], repm[:, 0:TH],
                         waits=[("cV", vr2)], track="dst1")
            vsp2 = P.dma("sync", st_dr[128:256, :], sq_t[:, 0:TH],
                         waits=[("cV", vr2)], track="dst1")
            gb = [A.bf16(TH + 1), A.bf16(TH + 1)]
            u_in = A.bf16(NT_D * (TH + 1))

            def u_t(j):
                return u_in[:, j * (TH + 1):j * (TH + 1) + TH]

            uv = [0] * NT_D
            for j in range(NT_D):
                w = [("cV", vr2)] + ([("cV", uv[j - 2])] if j >= 2 else [])
                vv = P.step("vector", lambda e, j=j: e.tensor_tensor(
                    out=gb[j % 2][:, 0:TH], in0=xb_t(j), in1=repm[:, 0:TH],
                    op=OP.subtract), waits=w, inc="cV")
                vv = P.step("vector", lambda e, j=j: e.scalar_tensor_tensor(
                    out=u_t(j), in0=gb[j % 2][:, 0:TH], scalar=col("ln1w", j),
                    in1=repi[:, 0:TH], op0=OP.mult, op1=OP.mult),
                    waits=[("cV", vv)], inc="cV")
                vv = P.step("vector", lambda e, j=j: e.tensor_scalar(
                    out=u_t(j), in0=u_t(j), scalar1=col("ln1b", j), scalar2=None,
                    op0=OP.add), waits=[("cV", vv)], inc="cV")
                uv[j] = P.step("vector", lambda e, j=j: e.tensor_scalar(
                    out=u_t(j)[:, 0:3], in0=u_t(j)[:, 0:3], scalar1=s_m3[:],
                    scalar2=None, op0=OP.mult), waits=[("cV", vv)], inc="cV")
            P.fence()
            A.pop()
            scratch0 = A.top
            _resv = A.f32(2 * TH + 512 + 2 * TH + (TH + 1))
            _u_in_resv = A.bf16(NT_D * (TH + 1))

            class Sub:
                def __init__(self, base, cols):
                    self.a, self.n, self.cols = base, 0, cols

                def f32(self, c):
                    v = self.a[:, self.n:self.n + c]
                    self.n += c
                    assert self.n <= self.cols
                    return v

                def bf16(self, c):
                    assert c % 2 == 0
                    v = self.a[:, self.n:self.n + c // 2].bitcast(BF16)
                    self.n += c // 2
                    assert self.n <= self.cols
                    return v

                def reset(self):
                    self.n = 0
            SA = Sub(arena_t[:, scratch0 - 0:scratch0], 0)
            SA = Sub(arena_t[:, 0:scratch0], scratch0)

            if upto <= 1:
                stop_dump(u_t(0)[:, 0:2048], u_t(15)[:, 0:2048])
            slab = [SA.bf16(2048), SA.bf16(2048)]
            xmb = [SA.bf16(TH + 1), SA.bf16(TH + 1)]
            diag = [SA.bf16(512), SA.bf16(512)]
            xcf = [SA.f32(512), SA.f32(512)]
            xcb = [SA.bf16(512), SA.bf16(512)]
            szb = [SA.f32(512), SA.f32(512)]

            slabv = [("dG", 0), ("dG", 0)]
            slab_used = [0, 0]
            xm_used = [0, 0]
            diag_used = [0, 0]
            xcf_used = [("xcf0", 0), ("xcf1", 0)]
            xcb_used = [("xcb0", 0), ("xcb1", 0)]
            sz_used = [("sz0", 0), ("sz1", 0)]
            bank_ev = [0] * 8
            for j in range(NT_H):
                if j in (8, 16, 24, 32):
                    qq = j // 8 - 1
                    P.step("gpsimd", lambda e, qq=qq: e.collective_compute(
                        "AllGather", OP.bypass, replica_groups=RG,
                        ins=[agi[qq * 1024:(qq + 1) * 1024]],
                        outs=[agoq[qq * NC * 1024:(qq + 1) * NC * 1024]]),
                        waits=[("xcb0", P.now("xcb0")),
                               ("xcb1", P.now("xcb1"))],
                        inc="cC")
                db = j % 2
                slabv[db] = slab_load(slab[db], w_ip, j,
                                      waits=[("cT", slab_used[db])],
                                      track=f"sl{db}")
                grid = TH_CH if j < 32 else T_CH
                off = 0 if j < 32 else 3
                for ci, (c0, cw) in enumerate(grid):
                    bk = (j * len(grid) + ci) % 2
                    for k in range(NT_D):
                        w = []
                        if k == 0:
                            w = [slabv[db], ("cA", bank_ev[bk])]
                        P.step("tensor", lambda e, db=db, k=k, c0=c0, cw=cw, bk=bk,
                               off=off: e.matmul(
                                   pb[bk][:, 0:cw], slab[db][:, k * 128:(k + 1) * 128],
                                   u_t(k)[:, off + c0:off + c0 + cw],
                                   start=(k == 0), stop=(k == NT_D - 1)),
                               waits=w, inc="cT" if k == NT_D - 1 else None)
                    vt = P.now("cT")
                    if ci == len(grid) - 1:
                        slab_used[db] = vt
                    if j < 32:
                        bank_ev[bk] = P.step(
                            "scalar", lambda e, db=db, c0=c0, cw=cw, bk=bk:
                            e.activation(out=xmb[db][:, c0:c0 + cw],
                                         in_=pb[bk][:, 0:cw], func=AF.Copy),
                            waits=[("cT", vt)] +
                            ([("cT", xm_used[db])] if ci == 0 else []),
                            inc="cA")
                    else:
                        va = P.step("scalar", lambda e, ci=ci, c0=c0, cw=cw, bk=bk:
                                    e.activation(out=szb[ci % 2][:, 0:cw],
                                                 in_=pb[bk][:, 0:cw], func=AF.Silu),
                                    waits=[("cT", vt), sz_used[ci % 2]],
                                    inc="cA")
                        bank_ev[bk] = va
                        sz_used[ci % 2] = P.dma(
                            "sync", sz_sp[(j - 32) * 128:(j - 31) * 128, c0:c0 + cw],
                            szb[ci % 2][:, 0:cw], waits=[("cA", va)],
                            track=f"sz{ci % 2}")
                if j < 32:
                    vd = 0
                    for kk in range(4):
                        vd = P.step("vector", lambda e, j=j, db=db, kk=kk:
                                    e.select(
                            out=diag[db][:, kk * 128:(kk + 1) * 128],
                            mask=s_identu[:],
                            on_true=s_cvw[:, j * 4 + kk:j * 4 + kk + 1]
                            .broadcast_to([128, 128]),
                            on_false=s_zb[:], add_drain=True),
                            waits=([("cT", diag_used[db])] if kk == 0 else
                                   [("cV", vd)]),
                            inc="cV")
                    xm_ready = P.now("cA")
                    for ci, (c0, cw) in enumerate(T_CH):
                        bk = 2 + (j * 4 + ci) % 2
                        for k in range(4):
                            w = []
                            if k == 0:
                                w = [("cV", vd), ("cA", bank_ev[bk]),
                                     ("cA", xm_ready)]
                            P.step("tensor", lambda e, db=db, k=k, c0=c0, cw=cw,
                                   bk=bk: e.matmul(
                                       pb[bk][:, 0:cw],
                                       diag[db][:, k * 128:(k + 1) * 128],
                                       xmb[db][:, c0 + k:c0 + k + cw],
                                       start=(k == 0), stop=(k == 3)),
                                   waits=w, inc="cT" if k == 3 else None)
                        vt = P.now("cT")
                        if ci == 3:
                            xm_used[db] = vt
                            diag_used[db] = vt
                        va = P.step("scalar", lambda e, j=j, ci=ci, cw=cw, bk=bk:
                                    e.activation(out=xcf[ci % 2][:, 0:cw],
                                                 in_=pb[bk][:, 0:cw], func=AF.Silu,
                                                 bias=col("cvb", j)),
                                    waits=[("cT", vt), xcf_used[ci % 2]],
                                    inc="cA")
                        bank_ev[bk] = va
                        va2 = P.step("scalar", lambda e, ci=ci, cw=cw: e.activation(
                            out=xcb[ci % 2][:, 0:cw], in_=xcf[ci % 2][:, 0:cw],
                            func=AF.Copy),
                            waits=[xcb_used[ci % 2], ("cA", va)],
                            inc="cA")
                        xcf_used[ci % 2] = P.dma(
                            "sync", xcf_sp[j * 128:(j + 1) * 128, c0:c0 + cw],
                            xcf[ci % 2][:, 0:cw], waits=[("cA", va)],
                            track=f"xcf{ci % 2}")
                        xcb_used[ci % 2] = P.dma(
                            "sync", agi[j * 128:(j + 1) * 128, c0:c0 + cw],
                            xcb[ci % 2][:, 0:cw], waits=[("cA", va2)],
                            track=f"xcb{ci % 2}")
            P.fence()

            if upto == 24:
                stop_dump(xcf_sp[31 * 128:32 * 128, :], agi[17 * 128:18 * 128, :])
            if upto <= 2:
                stop_dump(xcf_sp[0:128, :], sz_sp[0:128, :])
            SA = Sub(arena_t[:, 0:ARENA], ARENA)
            xslab = [SA.bf16(4096), SA.bf16(4096)]
            xcs = [SA.bf16(512), SA.bf16(512)]
            dtin = SA.bf16(2048)
            bcB = SA.bf16(2048)

            v0 = slab_load(xslab[0], w_xp, 0, track="dxp")
            v1 = slab_load(xslab[1], w_xp, 1, track="dxp")
            vxp = ("dxp", v1[1])
            xpe = [0, 0]
            xcs_guard = [0, 0]
            for ci, (c0, cw) in enumerate(T_CH):
                lmm = [0] * NT_E
                for k in range(NT_E):
                    vld = P.dma("sync", xcs[k % 2][:, 0:cw],
                                agi[k * 128:(k + 1) * 128, c0:c0 + cw],
                                waits=[("cT", xcs_guard[k % 2])],
                                track=f"xcs{k % 2}")
                    for m in range(2):
                        w = [vld]
                        if k == 0:
                            w += [vxp, ("cA", xpe[m])]
                        lmm[k] = P.step("tensor", lambda e, k=k, m=m, cw=cw: e.matmul(
                            pb[m][:, 0:cw], xslab[m][:, k * 128:(k + 1) * 128],
                            xcs[k % 2][:, 0:cw],
                            start=(k == 0), stop=(k == NT_E - 1)),
                            waits=w, inc="cT")
                    xcs_guard[k % 2] = lmm[k]
                vt = P.now("cT")
                xpe[0] = P.step("scalar", lambda e, c0=c0, cw=cw: e.activation(
                    out=dtin[:, c0:c0 + cw], in_=pb[0][:, 0:cw], func=AF.Copy),
                    waits=[("cT", vt)], inc="cA")
                xpe[1] = P.step("scalar", lambda e, c0=c0, cw=cw: e.activation(
                    out=bcB[0:32, c0:c0 + cw], in_=pb[1][0:32, 0:cw], func=AF.Copy),
                    inc="cA")
            vbc = P.dma("sync", bci[:, :], bcB[0:32, 0:T],
                        waits=[("cA", P.now("cA"))], track="dbc")

            if upto == 21:
                stop_dump(dtin[:, 0:T], bcB[0:32, 0:T])
            vdst = P.dma("sync", agi[E:E + 128, :], dtin[:, 0:T],
                         waits=[("cA", P.now("cA"))], track="agd0")
            if upto == 25:
                stop_dump(agi[0:128, :], agi[E:E + 128, :])
            P.step("gpsimd", lambda e: e.collective_compute(
                "AllGather", OP.bypass, replica_groups=RG,
                ins=[agi[E:E + 128]], outs=[agod]),
                waits=[vdst], inc="cC")
            P.step("gpsimd", lambda e: e.collective_compute(
                "AllGather", OP.bypass, replica_groups=RG, ins=[bci], outs=[bco]),
                waits=[vbc], inc="cC")
            vcoll = P.now("cC")
            P.dma_dyn("sync", bass.AP(myag.tensor, 0,
                                      [[512 * T, NC], [T, 512], [1, T]]),
                      lambda: bass.AP(agoq.tensor, hold["off2"] * T,
                                      [[1024 * T, NC], [T, 512], [1, T]]),
                      waits=[("cC", 4)], track="dmy")
            vgat = ("dmy", P.now("dmy"))
            P.fence()

            if upto <= 3:
                stop_dump(agi[0:128, :], bci[0:32, :])
            SA.reset()
            repB = [SA.bf16(2048) for _ in range(S)]
            repC = [SA.bf16(2048) for _ in range(S)]
            dtt = [SA.bf16(2048), SA.bf16(2048)]
            wct = [SA.bf16(2048), SA.bf16(2048)]
            dinb = SA.bf16(2048)
            dAb = [SA.f32(T), SA.f32(T)]
            bbf = SA.bf16(2048)
            hhb = [SA.bf16(2048), SA.bf16(2048)]
            gcb = SA.bf16(2048)
            ybo = SA.bf16(2048)
            sp_t = SA.f32(512)
            sp_e = SA.bf16(512)

            rB_used = [0] * S
            rC_used = [0] * S
            vB = [0] * S
            vC = [0] * S
            dtt_guard = [0, 0]
            wct_used = [0, 0]
            evac_tok = [0, 0]
            hcopy = [0] * 64
            h_used = [0, 0]
            dA_slot = [0, 0]
            tts_prev = 0
            gc_mm_last = 0
            dtmm_last = [0]
            spe_guard = [0]
            ln_last = [0]
            vln_j = [0, 0]
            vts_j = [0, 0]
            dtmm_j = [0, 0]
            vdl_h = [0]
            ybo_guard = ("ygs", 0)
            for cc in range(NC):
                def emit_dt(j, cc=cc):
                    jsl = j % 2
                    grp = 4 * jsl
                    for q in range(4):
                        w = []
                        if q == 0:
                            w = [vdl_h[0], ("d0g", 32), ("cA", evac_tok[jsl])]
                        vm = P.step(
                            "tensor", lambda e, j=j, q=q, grp=grp: e.matmul(
                                pb[grp + q][:, 0:512],
                                s_dtws[:, j * 128:(j + 1) * 128],
                                dinb[:, q * 512:(q + 1) * 512],
                                start=True, stop=True),
                            waits=w, inc="cT")
                        dtmm_j[jsl] = vm
                        dtmm_last[0] = vm
                        vts = P.step("vector", lambda e, j=j, q=q, grp=grp:
                                     e.tensor_scalar(
                                         out=sp_t[:, 0:512],
                                         in0=pb[grp + q][:, 0:512],
                                         scalar1=col("dtbs", j), scalar2=SPCL,
                                         op0=OP.add, op1=OP.min),
                                     waits=[("cT", vm),
                                            ("cA", spe_guard[0])], inc="cV")
                        vts_j[jsl] = vts
                        vex = P.step("scalar", lambda e: e.activation(
                            out=sp_e[:, 0:512], in_=sp_t[:, 0:512], func=AF.Exp),
                            waits=[("cV", vts), ("cA", ln_last[0])], inc="cA")
                        spe_guard[0] = vex
                        vln = P.step(
                            "scalar", lambda e, jsl=jsl, q=q: e.activation(
                                out=dtt[jsl][:, q * 512:(q + 1) * 512],
                                in_=sp_e[:, 0:512], func=AF.Ln, bias=1.0),
                            waits=[("cA", vex)] +
                                  ([("cA", dtt_guard[jsl])] if q == 0 else []),
                            inc="cA")
                        vln_j[jsl] = vln
                        ln_last[0] = vln

                vdl_h[0] = P.dma("sync", dinb[:, 0:T],
                                 agod[cc * 128:(cc + 1) * 128, :],
                                 waits=[("cC", 5), ("cT", dtmm_last[0])],
                                 track="dtl")
                for s in range(S):
                    vB[s] = P.dma("sync", repB[s][:, 0:T],
                                  bco[cc * 2 * S + s:cc * 2 * S + s + 1, :]
                                  .broadcast_to([128, T]),
                                  waits=[("cC", vcoll), ("cV", rB_used[s])],
                                  track=f"rB{s % 4}")
                    ceng = "sync" if s % 4 == 3 else "gpsimd"
                    vC[s] = P.dma(ceng, repC[s][:, 0:T],
                                  bco[cc * 2 * S + S + s:cc * 2 * S + S + s + 1, :]
                                  .broadcast_to([128, T]),
                                  waits=([("cC", vcoll)] if cc == 0 else []) +
                                        [("cV", rC_used[s])],
                                  track=f"rC{s % 4}")
                for j in range(4):
                    if j == 0:
                        emit_dt(0)
                        emit_dt(1)
                    jsl = j % 2
                    grp = 4 * jsl
                    vts_mine = vts_j[jsl]
                    dtmm_mine = dtmm_j[jsl]
                    vln_mine = vln_j[jsl]
                    sl = (cc * 4 + j) % 2
                    vxc = P.dma("sync", wct[sl][:, 0:T],
                                myag[cc * 512 + j * 128:cc * 512 + (j + 1) * 128, :],
                                waits=[vgat, ("cV", wct_used[sl])],
                                track=f"xcl{sl}")
                    vw = P.step("vector", lambda e, jsl=jsl, sl=sl: e.tensor_tensor(
                        out=wct[sl][:, 0:T], in0=dtt[jsl][:, 0:T],
                        in1=wct[sl][:, 0:T], op=OP.mult),
                        waits=[vxc, ("cA", vln_mine), ("cV", wct_used[sl])],
                        inc="cV")
                    vb = vw
                    for s in range(S):
                        hp = s % 2
                        col_sj = s * 4 + j
                        va = P.step("scalar", lambda e, jsl=jsl, j=j, s=s, hp=hp:
                                    e.activation(
                                        out=dAb[hp][:, 0:T],
                                        in_=dtt[jsl][:, 0:T],
                                        func=AF.Exp,
                                        scale=s_acol[:, j * S + s:j * S + s + 1]),
                                    waits=[("cA", vln_mine),
                                           ("cV", dA_slot[hp])],
                                    inc="cA")
                        vb = P.step("vector", lambda e, sl=sl, s=s:
                                    e.tensor_tensor(out=bbf[:, 0:T],
                                                    in0=wct[sl][:, 0:T],
                                                    in1=repB[s][:, 0:T],
                                                    op=OP.mult),
                                    waits=[("cV", vw), vB[s], ("cV", tts_prev)],
                                    inc="cV")
                        tts = P.step(
                            "vector", lambda e, hp=hp, col_sj=col_sj:
                            e.tensor_tensor_scan(
                                out=hhb[hp][:, 0:T], data0=dAb[hp][:, 0:T],
                                data1=bbf[:, 0:T],
                                initial=s_hst[:, col_sj:col_sj + 1],
                                op0=OP.mult, op1=OP.add),
                            waits=[("cA", va), ("cV", vb),
                                   ("cA", hcopy[col_sj]), ("cA", h_used[hp])],
                            inc="cV")
                        dA_slot[hp] = tts
                        tts_prev = tts
                        hcopy[col_sj] = P.step(
                            "scalar", lambda e, hp=hp, col_sj=col_sj: e.activation(
                                out=s_hst[:, col_sj:col_sj + 1],
                                in_=hhb[hp][:, T - 1:T], func=AF.Copy),
                            waits=[("cV", tts)], inc="cA")
                        h_used[hp] = hcopy[col_sj]
                        vgc = P.step("vector", lambda e, hp=hp, s=s:
                                     e.tensor_tensor(out=gcb[:, 0:T],
                                                     in0=hhb[hp][:, 0:T],
                                                     in1=repC[s][:, 0:T],
                                                     op=OP.mult),
                                     waits=[vC[s], ("cV", tts),
                                            ("cT", gc_mm_last)], inc="cV")
                        for q in range(4):
                            w = []
                            if q == 0:
                                w = [("cV", vgc), ("d0g", 32)]
                                if s == 0:
                                    w += [("cA", evac_tok[jsl]),
                                          ("cV", vts_mine),
                                          ("cT", dtmm_mine)]
                            gc_mm_last = P.step(
                                "tensor", lambda e, q=q, grp=grp, s=s: e.matmul(
                                    pb[grp + q][:, 0:512], s_identb[:],
                                    gcb[:, q * 512:(q + 1) * 512],
                                    start=(s == 0), stop=(s == S - 1)),
                                waits=w, inc="cT" if q == 3 else None)
                        rB_used[s] = vb
                        rC_used[s] = vgc
                    wct_used[sl] = vb
                    dtt_guard[jsl] = P.now("cA")
                    vev = 0
                    for q in range(4):
                        vev = P.step("scalar", lambda e, q=q, grp=grp:
                                     e.activation(
                                         out=ybo[:, q * 512:(q + 1) * 512],
                                         in_=pb[grp + q][:, 0:512],
                                         func=AF.Copy),
                                     waits=([("cT", gc_mm_last), ybo_guard]
                                            if q == 0 else []),
                                     inc="cA")
                    evac_tok[jsl] = vev
                    ybo_guard = P.dma(
                        "sync", ygi[cc * 512 + j * 128:cc * 512 + (j + 1) * 128, :],
                        ybo[:, 0:T], waits=[("cA", vev)], track="ygs")
                    if j + 2 < 4:
                        emit_dt(j + 2)
                P.step("gpsimd", lambda e, cc=cc: e.collective_compute(
                    "AllGather", OP.bypass, replica_groups=RG,
                    ins=[ygi[cc * 512:(cc + 1) * 512]],
                    outs=[ygo[cc * NC * 512:(cc + 1) * NC * 512]]),
                    waits=[("ygs", P.now("ygs"))], inc="cC")
            vcoll2 = P.now("cC")
            P.fence()

            if upto <= 4:
                stop_dump(ygi[0:128, :], ygi[512:640, :])
            SA.reset()
            _ysb = SA.bf16(4 * 2048)
            ysblk = [_ysb, _ysb]
            _xct = SA.f32(T)
            xct = [_xct, _xct]
            _szt = SA.f32(T)
            szt = [_szt, _szt]
            yab = SA.f32(T)
            ytile = [SA.bf16(2048), SA.bf16(2048)]
            yhalf = SA.bf16(NT_E * 1024)
            oslab = [SA.bf16(4096), SA.bf16(4096)]
            xfc = [SA.f32(512), SA.f32(512)]
            u0b = SA.f32(512)
            tcb = [SA.f32(512), SA.f32(512)]
            repm0 = SA.f32(T)
            repi0 = SA.f32(T)

            P.dma_dyn("sync", bass.AP(myy.tensor, 0, [[T, E], [1, T]]),
                      lambda: bass.AP(ygo.tensor, hold["off"] * (8 * T),
                                      [[T, E], [1, T]]),
                      waits=[("cC", vcoll2)], track="dmyy")
            vygat = ("dmyy", P.now("dmyy"))

            yprev = [0, 0]
            yt_used = [("ysp0", 0), ("ysp1", 0)]
            blk_used = [0, 0]
            for b in range(NC):
                vblk = P.dma(
                    "sync", ysblk[b % 2].rearrange("p (j t) -> p j t", j=4),
                    myy[b * 512:(b + 1) * 512, :].rearrange("(j p) t -> p j t", p=128),
                    waits=[vygat, ("cV", blk_used[b % 2])], track="ysb")
                for jj in range(4):
                    je = b * 4 + jj
                    yst_ap = ysblk[b % 2][:, jj * T:jj * T + T]
                    v2_ = P.dma("sync", xct[je % 2][:, 0:T],
                                xcf_sp[je * 128:(je + 1) * 128, :],
                                waits=[("cV", yprev[je % 2])],
                                track=f"gxc{je % 2}")
                    v3_ = P.dma("sync", szt[je % 2][:, 0:T],
                                sz_sp[je * 128:(je + 1) * 128, :],
                                waits=[("cV", yprev[je % 2])],
                                track=f"gsz{je % 2}")
                    vya = P.step("vector", lambda e, je=je, yst_ap=yst_ap:
                                 e.scalar_tensor_tensor(
                        out=yab[:, 0:T], in0=xct[je % 2][:, 0:T],
                        scalar=col("dsk", je), in1=yst_ap,
                        op0=OP.mult, op1=OP.add),
                        waits=[v2_, v3_, vblk, ("cV", yprev[1 - je % 2])],
                        inc="cV")
                    yprev[je % 2] = P.step("vector", lambda e, je=je: e.tensor_tensor(
                        out=ytile[je % 2][:, 0:T], in0=yab[:, 0:T],
                        in1=szt[je % 2][:, 0:T], op=OP.mult),
                        waits=[yt_used[je % 2], ("cV", vya)], inc="cV")
                    yt_used[je % 2] = P.dma("sync", y_sp[je * 128:(je + 1) * 128, :],
                                            ytile[je % 2][:, 0:T],
                                            waits=[("cV", yprev[je % 2])],
                                            track=f"ysp{je % 2}")
                    if jj == 3:
                        blk_used[b % 2] = yprev[je % 2]
            vm0 = P.dma("sync", repm0[:, 0:T], st_dr[0:128, 3:3 + T],
                        waits=[("dst1", P.now("dst1"))], track="dst2")
            vi0 = P.dma("sync", repi0[:, 0:T], st_dr[128:256, 3:3 + T],
                        waits=[("dst1", P.now("dst1"))], track="dst2")
            vm0 = vi0 = ("dst2", P.now("dst2"))

            tspv = []
            oslab_used = [0, 0]
            xfc_used = [0, 0]
            u0b_used = 0
            tcb_used = [("tsp0", 0), ("tsp1", 0)]
            bank_ev = [0] * 8
            for hf in range(2):
                hb = hf * 1024
                vy = 0
                for k in range(NT_E):
                    vy = P.dma("sync", yhalf[:, k * 1024:(k + 1) * 1024],
                               y_sp[k * 128:(k + 1) * 128, hb:hb + 1024],
                               waits=([("cT", oslab_used[0]), ("cT", oslab_used[1]),
                                       ("ysp0", P.now("ysp0")),
                                       ("ysp1", P.now("ysp1"))]
                                      if k == 0 else []), track="yhf",
                               selfwait=False)
                vy = ("yhf", P.now("yhf"))
                for m in range(NT_D):
                    mb = m % 2
                    vsl = slab_load(oslab[mb], w_op, m,
                                    waits=[("cT", oslab_used[mb])],
                                    track=f"osl{mb}")
                    for n2 in range(2):
                        c0 = hb + n2 * 512
                        bk = (m * 2 + n2) % 2
                        for k in range(NT_E):
                            w = [vy] if k == 0 else []
                            if k == 0:
                                w += [vsl, ("cV", bank_ev[bk])]
                            P.step("tensor", lambda e, mb=mb, k=k, n2=n2, bk=bk:
                                   e.matmul(pb[bk][:, 0:512],
                                            oslab[mb][:, k * 128:(k + 1) * 128],
                                            yhalf[:, k * 1024 + n2 * 512:
                                                  k * 1024 + n2 * 512 + 512],
                                            start=(k == 0), stop=(k == NT_E - 1)),
                                   waits=w, inc="cT" if k == NT_E - 1 else None)
                        vt = P.now("cT")
                        if n2 == 1:
                            oslab_used[mb] = vt
                        vx = P.dma("sync", xfc[n2][:, 0:512],
                                   xfm[m * 128:(m + 1) * 128, 3 + c0:3 + c0 + 512],
                                   waits=[("cV", xfc_used[n2])],
                                   track=f"xfc{n2}")
                        vu = P.step(
                            "vector", lambda e, n2=n2, c0=c0: e.tensor_tensor(
                                out=u0b[:, 0:512], in0=xfc[n2][:, 0:512],
                                in1=repm0[:, c0:c0 + 512], op=OP.subtract),
                            waits=[vx, vm0, ("cV", u0b_used)], inc="cV")
                        xfc_used[n2] = vu
                        vu = P.step("vector", lambda e, c0=c0: e.tensor_tensor(
                            out=u0b[:, 0:512], in0=u0b[:, 0:512],
                            in1=repi0[:, c0:c0 + 512], op=OP.mult),
                            waits=[vi0, ("cV", vu)], inc="cV")
                        vv = P.step("vector", lambda e, n2=n2, bk=bk: e.tensor_tensor(
                            out=tcb[n2][:, 0:512], in0=u0b[:, 0:512],
                            in1=pb[bk][:, 0:512], op=OP.add),
                            waits=[("cT", vt), tcb_used[n2], ("cV", vu)],
                            inc="cV")
                        u0b_used = vv
                        bank_ev[bk] = vv
                        tcb_used[n2] = P.dma(
                            "sync", t_sp[m * 128:(m + 1) * 128, c0:c0 + 512],
                            tcb[n2][:, 0:512], waits=[("cV", vv)],
                            track=f"tsp{n2}")
                        tspv.append(tcb_used[n2])
            P.fence()

            if upto <= 5:
                stop_dump(t_sp[0:128, :], y_sp[0:128, :])
            SA.reset()
            tb = [SA.bf16(2048), SA.bf16(2048)]
            sqb2 = [SA.bf16(2048), SA.bf16(2048)]
            ln2o = SA.bf16(NT_D * 2048)

            def l2t(k):
                return ln2o[:, k * 2048:k * 2048 + T]

            def t_load(j, buf, w):
                return P.dma("gpsimd", buf[:, 0:T],
                             t_sp[j * 128:(j + 1) * 128, :],
                             waits=list(w) + [("tsp0", P.now("tsp0")),
                                              ("tsp1", P.now("tsp1"))],
                             track=f"tld{j % 2}")
            repm2 = SA.f32(T)
            repi2 = SA.f32(T)
            sum2 = SA.f32(T)
            sq2 = SA.f32(T)
            sdone = stats_streamed(t_load, sqb2, tb, sum2, sq2)
            vr2 = ln_math(sdone, sum2, sq2, repm2, repi2, T, D)
            gb2 = [SA.bf16(2048), SA.bf16(2048)]
            lv = [0, 0]
            vstats_mm = P.now("cT")
            for j in range(NT_D):
                v = P.dma("gpsimd", tb[j % 2][:, 0:T], t_sp[j * 128:(j + 1) * 128, :],
                          waits=[("cV", lv[j % 2])] +
                                ([("cT", vstats_mm)] if j < 2 else []),
                          track=f"tld{j % 2}")
                vv0 = P.step("vector", lambda e, j=j: e.tensor_tensor(
                    out=gb2[j % 2][:, 0:T], in0=tb[j % 2][:, 0:T],
                    in1=repm2[:, 0:T], op=OP.subtract),
                    waits=[v, ("cV", vr2)], inc="cV")
                vv0 = P.step("vector", lambda e, j=j: e.scalar_tensor_tensor(
                    out=l2t(j), in0=gb2[j % 2][:, 0:T], scalar=col("ln2w", j),
                    in1=repi2[:, 0:T], op0=OP.mult, op1=OP.mult),
                    waits=[("cV", vv0)], inc="cV")
                lv[j % 2] = P.step("vector", lambda e, j=j: e.tensor_scalar(
                    out=l2t(j), in0=l2t(j), scalar1=col("ln2b", j), scalar2=None,
                    op0=OP.add), waits=[("cV", vv0)], inc="cV")
            vln2 = P.now("cV")

            fslab = [SA.bf16(2048) for _ in range(4)]
            sgb = [SA.bf16(512), SA.bf16(512)]
            hgb = [SA.bf16(512), SA.bf16(512)]
            fs_used = [0, 0, 0, 0]
            hg_used = [("hgs0", 0), ("hgs1", 0)]
            sg_used = [0, 0]
            bank_ev = [0] * 8
            for jp in range(32):
                sa, sg_ = jp % 2, 2 + jp % 2
                va_ = slab_load(fslab[sa], w_fp, jp, waits=[("cT", fs_used[sa])],
                                track=f"fsl{sa}")
                vg_ = slab_load(fslab[sg_], w_fp, jp + 32,
                                waits=[("cT", fs_used[sg_])], track=f"fsl{sg_}")
                for ci, (c0, cw) in enumerate(T_CH):
                    bka, bkg = 0, 1
                    for k in range(NT_D):
                        w = [("cV", vln2)] if (jp == 0 and ci == 0 and k == 0) else []
                        if k == 0:
                            w += [va_, ("cV", bank_ev[bka])]
                        P.step("tensor", lambda e, sa=sa, k=k, c0=c0, cw=cw, bka=bka:
                               e.matmul(pb[bka][:, 0:cw],
                                        fslab[sa][:, k * 128:(k + 1) * 128],
                                        l2t(k)[:, c0:c0 + cw],
                                        start=(k == 0), stop=(k == NT_D - 1)),
                               waits=w, inc="cT" if k == NT_D - 1 else None)
                    vta = P.now("cT")
                    for k in range(NT_D):
                        w = []
                        if k == 0:
                            w = [vg_, ("cA", bank_ev[bkg])]
                        P.step("tensor", lambda e, sg_=sg_, k=k, c0=c0, cw=cw, bkg=bkg:
                               e.matmul(pb[bkg][:, 0:cw],
                                        fslab[sg_][:, k * 128:(k + 1) * 128],
                                        l2t(k)[:, c0:c0 + cw],
                                        start=(k == 0), stop=(k == NT_D - 1)),
                               waits=w, inc="cT" if k == NT_D - 1 else None)
                    vtg = P.now("cT")
                    if ci == 3:
                        fs_used[sa] = vta
                        fs_used[sg_] = vtg
                    vsg = P.step("scalar", lambda e, ci=ci, cw=cw: e.activation(
                        out=sgb[ci % 2][:, 0:cw], in_=pb[1][:, 0:cw], func=AF.Silu),
                        waits=[("cT", vtg), ("cV", sg_used[ci % 2])], inc="cA")
                    bank_ev[1] = vsg
                    vhg = P.step("vector", lambda e, ci=ci, cw=cw: e.tensor_tensor(
                        out=hgb[ci % 2][:, 0:cw], in0=pb[0][:, 0:cw],
                        in1=sgb[ci % 2][:, 0:cw], op=OP.mult),
                        waits=[("cA", vsg), hg_used[ci % 2]], inc="cV")
                    sg_used[ci % 2] = vhg
                    bank_ev[0] = vhg
                    hg_used[ci % 2] = P.dma(
                        "sync", hg_sp[jp * 128:(jp + 1) * 128, c0:c0 + cw],
                        hgb[ci % 2][:, 0:cw], waits=[("cV", vhg)],
                        track=f"hgs{ci % 2}")
            P.fence()

            if upto <= 6:
                stop_dump(hg_sp[0:128, :], hg_sp[128:256, :])
            SA.reset()
            hghalf = SA.bf16(NT_E * 1024)
            sum3 = SA.f32(T)
            sq3 = SA.f32(T)
            foslab = [SA.bf16(4096), SA.bf16(4096)]
            tcc = [SA.f32(512), SA.f32(512)]
            atf = SA.f32(512)
            atb = [SA.bf16(512), SA.bf16(512)]
            sqb3 = [SA.bf16(512), SA.bf16(512)]

            fo_used = [0, 0]
            at_used = [("abs0", 0), ("abs1", 0)]
            tcc_used = [0, 0]
            sq_guard = 0
            bank_ev = [0] * 8
            for hf in range(2):
                hb = hf * 1024
                vy = 0
                for k in range(NT_E):
                    vy = P.dma("sync", hghalf[:, k * 1024:(k + 1) * 1024],
                               hg_sp[k * 128:(k + 1) * 128, hb:hb + 1024],
                               waits=[("cT", fo_used[0]), ("cT", fo_used[1])]
                               if k == 0 else [])
                for m in range(NT_D):
                    mb = m % 2
                    vsl = slab_load(foslab[mb], w_fo, m, waits=[("cT", fo_used[mb])])
                    for n2 in range(2):
                        c0 = hb + n2 * 512
                        ci_g = hf * 2 + n2
                        bk = n2 % 2
                        for k in range(NT_E):
                            w = [vy] if k == 0 else []
                            if k == 0:
                                w += [vsl, ("cV", bank_ev[bk])]
                            P.step("tensor", lambda e, mb=mb, k=k, n2=n2, bk=bk:
                                   e.matmul(pb[bk][:, 0:512],
                                            foslab[mb][:, k * 128:(k + 1) * 128],
                                            hghalf[:, k * 1024 + n2 * 512:
                                                   k * 1024 + n2 * 512 + 512],
                                            start=(k == 0), stop=(k == NT_E - 1)),
                                   waits=w, inc="cT" if k == NT_E - 1 else None)
                        vt = P.now("cT")
                        if n2 == 1:
                            fo_used[mb] = vt
                        vtl = P.dma("sync", tcc[n2][:, 0:512],
                                    t_sp[m * 128:(m + 1) * 128, c0:c0 + 512],
                                    waits=[("cV", tcc_used[n2])],
                                    track=f"tcc{n2}")
                        vv = P.step("vector", lambda e, n2=n2, bk=bk: e.tensor_tensor(
                            out=atf[:, 0:512], in0=tcc[n2][:, 0:512],
                            in1=pb[bk][:, 0:512], op=OP.add),
                            waits=[("cT", vt), vtl], inc="cV")
                        bank_ev[bk] = vv
                        tcc_used[n2] = vv
                        va = P.step("scalar", lambda e, n2=n2: e.activation(
                            out=atb[n2][:, 0:512], in_=atf[:, 0:512], func=AF.Copy),
                            waits=[("cV", vv), at_used[n2]], inc="cA")
                        at_used[n2] = P.dma(
                            "sync", ab_sp[m * 128:(m + 1) * 128, c0:c0 + 512],
                            atb[n2][:, 0:512], waits=[("cA", va)],
                            track=f"abs{n2}")
                        P.step("tensor", lambda e, n2=n2, ci_g=ci_g, m=m: e.matmul(
                            pb[2 + ci_g][:, 0:512], s_onef[:], atb[n2][:, 0:512],
                            start=(m == 0), stop=(m == NT_D - 1)),
                            waits=[("cA", va)] +
                                  ([("cA", sq_guard)] if m == 0 else []),
                            inc="cT" if m == NT_D - 1 else None)
                        vq = P.step("scalar", lambda e, n2=n2: e.activation(
                            out=sqb3[n2][:, 0:512], in_=atb[n2][:, 0:512],
                            func=AF.Square), waits=[("cA", va)], inc="cA")
                        P.step("tensor", lambda e, n2=n2, m=m: e.matmul(
                            pb[6 + n2][:, 0:512], s_onef[:], sqb3[n2][:, 0:512],
                            start=(m == 0), stop=(m == NT_D - 1)),
                            waits=[("cA", vq)] +
                                  ([("cA", sq_guard)] if m == 0 else []),
                            inc="cT" if m == NT_D - 1 else None)
                vt = P.now("cT")
                for n2 in range(2):
                    ci_g = hf * 2 + n2
                    c0 = hb + n2 * 512
                    P.step("scalar", lambda e, ci_g=ci_g, c0=c0: e.activation(
                        out=sum3[:, c0:c0 + 512], in_=pb[2 + ci_g][:, 0:512],
                        func=AF.Copy), waits=[("cT", vt)], inc="cA")
                    P.step("scalar", lambda e, n2=n2, c0=c0: e.activation(
                        out=sq3[:, c0:c0 + 512], in_=pb[6 + n2][:, 0:512],
                        func=AF.Copy), inc="cA")
                sq_guard = P.now("cA")
            P.fence()
            sdone = P.now("cA")
            repm3 = SA.f32(T)
            repi3 = SA.f32(T)
            vr3 = ln_math(sdone, sum3, sq3, repm3, repi3, T, D)
            _abt = SA.bf16(2048)
            abt = [_abt, _abt]
            xft = [SA.f32(T), SA.f32(T)]
            lnb = SA.f32(T)
            x1c = [SA.f32(T), SA.f32(T)]
            x1b = [SA.bf16(2048), SA.bf16(2048)]
            sqb4 = [SA.bf16(512), SA.bf16(512)]
            x1_used = [0, 0]
            x1mm = [0] * NT_D
            sqlast = [0, 0]
            sqv = [0, 0]
            for j in range(NT_D):
                jb = j % 2
                v1_ = P.dma("sync", abt[jb][:, 0:T], ab_sp[j * 128:(j + 1) * 128, :],
                            waits=[("cA", x1_used[jb]),
                                   ("abs0", P.now("abs0")),
                                   ("abs1", P.now("abs1"))], track="abl")
                v2_ = P.dma("sync", xft[jb][:, 0:T],
                            xfm[j * 128:(j + 1) * 128, 3:3 + T],
                            waits=[("cA", x1_used[jb])], track=f"xft{jb}")
                vv = P.step("vector", lambda e, jb=jb: e.tensor_tensor(
                    out=lnb[:, 0:T], in0=abt[jb][:, 0:T], in1=repm3[:, 0:T],
                    op=OP.subtract), waits=[v1_, ("cV", vr3)], inc="cV")
                vv = P.step("vector", lambda e, jb=jb: e.tensor_tensor(
                    out=lnb[:, 0:T], in0=lnb[:, 0:T], in1=repi3[:, 0:T],
                    op=OP.mult), waits=[("cV", vv)], inc="cV")
                vv = P.step("vector", lambda e, jb=jb: e.tensor_tensor(
                    out=x1c[jb][:, 0:T], in0=xft[jb][:, 0:T], in1=lnb[:, 0:T],
                    op=OP.add),
                    waits=[v2_, ("cV", vv), ("cA", x1_used[jb]),
                           (f"x1s{jb}", P.now(f"x1s{jb}"))],
                    inc="cV")
                P.dma("sync", x1_sp[j * 128:(j + 1) * 128, :], x1c[jb][:, 0:T],
                      waits=[("cV", vv)], track=f"x1s{jb}")
                va = P.step("scalar", lambda e, jb=jb: e.activation(
                    out=x1b[jb][:, 0:T], in_=x1c[jb][:, 0:T], func=AF.Copy),
                    waits=[("cV", vv)] + ([("cT", x1mm[j - 2])] if j >= 2 else []),
                    inc="cA")
                x1_used[jb] = va
                for ci, (c0, cw) in enumerate(T_CH):
                    P.step("tensor", lambda e, jb=jb, c0=c0, cw=cw, ci=ci, j=j:
                           e.matmul(pb[ci][:, 0:cw], s_onef[:],
                                    x1b[jb][:, c0:c0 + cw],
                                    start=(j == 0), stop=(j == NT_D - 1)),
                           waits=[("cA", va)] if ci == 0 else [],
                           inc=None)
                    sl = ci % 2
                    vq = P.step("scalar", lambda e, jb=jb, sl=sl, c0=c0, cw=cw:
                        e.activation(
                        out=sqb4[sl][:, 0:cw], in_=x1b[jb][:, c0:c0 + cw],
                        func=AF.Square),
                        waits=[("cA", va), ("cA", sqv[sl]), ("cT", sqlast[sl])],
                        inc="cA")
                    sqv[sl] = vq
                    sqlast[sl] = P.step(
                        "tensor", lambda e, sl=sl, cw=cw, ci=ci, j=j: e.matmul(
                            pb[4 + ci][:, 0:cw], s_onef[:], sqb4[sl][:, 0:cw],
                            start=(j == 0), stop=(j == NT_D - 1)),
                        waits=[("cA", vq)], inc="cT")
                    x1mm[j] = sqlast[sl]
            vt = P.now("cT")
            for ci, (c0, cw) in enumerate(T_CH):
                P.step("scalar", lambda e, c0=c0, cw=cw, ci=ci: e.activation(
                    out=sum3[:, c0:c0 + cw], in_=pb[ci][:, 0:cw], func=AF.Copy),
                    waits=[("cT", vt)] if ci == 0 else [], inc="cA")
                P.step("scalar", lambda e, c0=c0, cw=cw, ci=ci: e.activation(
                    out=sq3[:, c0:c0 + cw], in_=pb[4 + ci][:, 0:cw], func=AF.Copy),
                    inc="cA")
            sdone = P.now("cA")
            x1sum_sp = P.dma("sync", st_dr[0:128, 0:T], sum3[:, 0:T],
                             waits=[("cA", sdone)])
            x1sq_sp = P.dma("sync", st_dr[128:256, 0:T], sq3[:, 0:T],
                            waits=[("cA", sdone)])
            P.fence()

            if upto <= 7:
                stop_dump(x1_sp[0:128, :], ab_sp[0:128, :])
            SA.reset()
            repm4 = SA.f32(T)
            repi4 = SA.f32(T)
            sum4 = SA.f32(T)
            sq4 = SA.f32(T)
            vld1 = P.dma("sync", sum4[:, 0:T], st_dr[0:128, 0:T],
                         waits=[("dst3", P.now("dst3"))], track="dst4")
            vld2 = P.dma("sync", sq4[:, 0:T], st_dr[128:256, 0:T],
                         waits=[("dst3", P.now("dst3"))], track="dst4")
            _v4 = P.now("dst4")
            P.q["scalar"].append(lambda e: e.wait_ge(P.sems["dst4"], _v4))
            P.q["vector"].append(lambda e: e.wait_ge(P.sems["dst4"], _v4))
            vr4 = ln_math(P.now("cA"), sum4, sq4, repm4, repi4, T, D)
            x1t = [SA.bf16(2048), SA.bf16(2048)]
            gb4 = SA.bf16(2048)
            u1 = SA.bf16(NT_D * 2048)

            def u1t(k):
                return u1[:, k * 2048:k * 2048 + T]

            lv = [0, 0]
            for j in range(NT_D):
                jb = j % 2
                v = P.dma("gpsimd", x1t[jb][:, 0:T], x1_sp[j * 128:(j + 1) * 128, :],
                          waits=[("cV", lv[jb]), ("x1s0", P.now("x1s0")),
                                 ("x1s1", P.now("x1s1"))], track=f"x1l{jb}")
                vv0 = P.step("vector", lambda e, jb=jb: e.tensor_tensor(
                    out=gb4[:, 0:T], in0=x1t[jb][:, 0:T], in1=repm4[:, 0:T],
                    op=OP.subtract),
                    waits=[v, ("cV", vr4), ("cV", lv[1 - jb])], inc="cV")
                lv[jb] = P.step("vector", lambda e, j=j, jb=jb: e.tensor_tensor(
                    out=u1t(j), in0=gb4[:, 0:T], in1=repi4[:, 0:T], op=OP.mult),
                    waits=[("cV", vv0)], inc="cV")
            vu1 = P.now("cV")
            repal = SA.f32(D)
            val = P.dma("gpsimd", repal[:, 0:D],
                        al_dr.rearrange("a b -> (a b)").unsqueeze(0)
                        .broadcast_to([128, D]), track="dra")
            fnslab = [SA.bf16(2048), SA.bf16(2048)]
            sgn = SA.bf16(2048)
            bwslab = [SA.bf16(2048), SA.bf16(2048)]
            ob = [SA.bf16(512), SA.bf16(512)]
            fn_used = [0, 0]
            last_bw = 0
            ob_used = [("obs0", 0), ("obs1", 0)]
            bank_ev = [0] * 8
            for m in range(NT_D):
                mb = m % 2
                vsl = slab_load(fnslab[mb], w_fnt, m,
                                waits=[("cV", fn_used[mb])], track=f"fnl{mb}")
                va = P.step("scalar", lambda e, mb=mb: e.activation(
                    out=sgn[:, 0:2048], in_=fnslab[mb][:, 0:2048], func=AF.Sign),
                    waits=[vsl, ("cV", last_bw)], inc="cA")
                last_bw = P.step("vector", lambda e, mb=mb, m=m: e.tensor_tensor(
                    out=bwslab[mb][:, 0:2048].rearrange("p (a b) -> p a b", b=128),
                    in0=sgn[:, 0:2048].rearrange("p (a b) -> p a b", b=128),
                    in1=repal[:, m * 128:(m + 1) * 128].unsqueeze(1)
                    .broadcast_to([128, NT_D, 128]),
                    op=OP.mult), waits=[("cA", va), val], inc="cV")
                fn_used[mb] = last_bw
                for ci, (c0, cw) in enumerate(T_CH):
                    bk = ci % 2
                    for k in range(NT_D):
                        w = [("cV", vu1)] if (m == 0 and ci == 0 and k == 0) else []
                        if k == 0:
                            w += [("cV", fn_used[mb]), ("cV", bank_ev[bk])]
                        P.step("tensor", lambda e, mb=mb, k=k, c0=c0, cw=cw, bk=bk:
                               e.matmul(pb[bk][:, 0:cw],
                                        bwslab[mb][:, k * 128:(k + 1) * 128],
                                        u1t(k)[:, c0:c0 + cw],
                                        start=(k == 0), stop=(k == NT_D - 1)),
                               waits=w, inc="cT" if k == NT_D - 1 else None)
                    vt = P.now("cT")
                    va = P.step("vector", lambda e, m=m, ci=ci, cw=cw, bk=bk:
                                e.tensor_scalar(out=ob[ci % 2][:, 0:cw],
                                                in0=pb[bk][:, 0:cw],
                                                scalar1=col("fnb", m), scalar2=None,
                                                op0=OP.add),
                                waits=[("cT", vt), ob_used[ci % 2]],
                                inc="cV")
                    bank_ev[bk] = va
                    ob_used[ci % 2] = P.dma(
                        "sync", o_sp[m * 128:(m + 1) * 128, c0:c0 + cw],
                        ob[ci % 2][:, 0:cw], waits=[("cV", va)])
            P.fence()

            if upto <= 8:
                stop_dump(o_sp[0:128, :], o_sp[128:256, :])
            SA.reset()
            obt = [SA.bf16(2048), SA.bf16(2048)]
            sqb5 = [SA.bf16(2048), SA.bf16(2048)]

            def o_load(j, buf, w):
                return P.dma("gpsimd", buf[:, 0:T], o_sp[j * 128:(j + 1) * 128, :],
                             waits=w, track=f"old{j % 2}")
            repm5 = SA.f32(T)
            repi5 = SA.f32(T)
            sum5 = SA.f32(T)
            sq5 = SA.f32(T)
            sdone = stats_streamed(o_load, sqb5, obt, sum5, sq5)
            vr5 = ln_math(sdone, sum5, sq5, repm5, repi5, T, D)
            x1f = [SA.f32(T), SA.f32(T)]
            lnf = SA.f32(T)
            outc = [SA.f32(T), SA.f32(T)]
            out_used = [("os0", 0), ("os1", 0)]
            vstats_mm = P.now("cT")
            lastadd = 0
            for j in range(NT_D):
                jb = j % 2
                v1_ = P.dma("gpsimd", obt[jb][:, 0:T], o_sp[j * 128:(j + 1) * 128, :],
                            waits=[out_used[jb]] +
                                  ([("cT", vstats_mm)] if j < 2 else []),
                            track=f"olb{jb}")
                v2_ = P.dma("sync", x1f[jb][:, 0:T], x1_sp[j * 128:(j + 1) * 128, :],
                            waits=[out_used[jb]], track=f"x1f{jb}")
                vv = P.step("vector", lambda e, jb=jb: e.tensor_tensor(
                    out=lnf[:, 0:T], in0=obt[jb][:, 0:T], in1=repm5[:, 0:T],
                    op=OP.subtract),
                    waits=[v1_, ("cV", vr5), ("cV", lastadd)], inc="cV")
                vv = P.step("vector", lambda e, jb=jb: e.tensor_tensor(
                    out=lnf[:, 0:T], in0=lnf[:, 0:T], in1=repi5[:, 0:T],
                    op=OP.mult), waits=[("cV", vv)], inc="cV")
                vv = P.step("vector", lambda e, jb=jb: e.tensor_tensor(
                    out=outc[jb][:, 0:T], in0=x1f[jb][:, 0:T], in1=lnf[:, 0:T],
                    op=OP.add), waits=[v2_, ("cV", vv)], inc="cV")
                lastadd = vv
                out_used[jb] = P.dma("sync", out[j * 128:(j + 1) * 128, :],
                                     outc[jb][:, 0:T], waits=[("cV", vv)],
                                     track=f"os{jb}")
            P.fence()


        except _Stop:
            pass
        block = st.enter_context(nc.Block())
        P.emit(block)
    return nc


def _slabs(w):
    K, M = w.shape
    return np.ascontiguousarray(
        w.reshape(K, M // 128, 128).transpose(1, 0, 2))


def _cols(v):
    n = v.shape[0]
    return np.ascontiguousarray(v.reshape(n // 128, 128).T)


def stage(inputs):
    f32 = np.float32
    x = np.asarray(inputs["L0_attn_x"], f32)[0]
    ipw = np.asarray(inputs["L0_attn_in_proj_w"], f32)
    cvw = np.asarray(inputs["L0_attn_conv_w"], f32).reshape(E, 4)
    cvb = np.asarray(inputs["L0_attn_conv_b"], f32)
    xpw = np.asarray(inputs["L0_attn_x_proj_w"], f32)
    dtw = np.asarray(inputs["L0_attn_dt_proj_w"], f32)
    dtb = np.asarray(inputs["L0_attn_dt_proj_b"], f32)
    alog = np.asarray(inputs["L0_attn_A_log"], f32)
    dsk = np.asarray(inputs["L0_attn_D_skip"], f32)
    opw = np.asarray(inputs["L0_attn_out_proj_w"], f32)
    fpw = np.asarray(inputs["L0_attn_ff_proj_w"], f32)
    fow = np.asarray(inputs["L0_attn_ff_out_w"], f32)
    fnw = np.asarray(inputs["L0_ffn_weight"], f32)
    fnb = np.asarray(inputs["L0_ffn_bias"], f32)

    com = dict(
        w_ip=_slabs(np.ascontiguousarray(ipw.T)),
        w_xp=_slabs(np.concatenate(
            [np.ascontiguousarray(xpw.T),
             np.zeros((E, 96), f32)], axis=1)),

        w_op=_slabs(np.ascontiguousarray(opw.T)),
        w_fp=_slabs(np.ascontiguousarray(fpw.T)),
        w_fo=_slabs(np.ascontiguousarray(fow.T)),
        w_fnt=_slabs(np.ascontiguousarray(fnw.T)),
        w_fnr=fnw,
        cvw=cvw, cvb=_cols(cvb), dsk=_cols(dsk),
        ln1w=_cols(np.asarray(inputs["L0_attn_ln1_w"], f32)),
        ln1b=_cols(np.asarray(inputs["L0_attn_ln1_b"], f32)),
        ln2w=_cols(np.asarray(inputs["L0_attn_ln2_w"], f32)),
        ln2b=_cols(np.asarray(inputs["L0_attn_ln2_b"], f32)),
        fnb=_cols(fnb),
        ident=np.eye(128, dtype=f32),
        identu=np.eye(128, dtype=np.uint8),
        onecol=np.ones((128, 1), ml_dtypes.bfloat16),
        onefull=np.ones((128, 128), ml_dtypes.bfloat16),
        zcol=np.zeros((128, 64), f32),
        zb=np.zeros((128, 128), ml_dtypes.bfloat16),
        cvals=np.broadcast_to(
            np.array([1e-5, 1.0, 0.0, 0.0], f32), (128, 4)).copy(),
    )
    in_maps = []
    for c in range(NC):
        lo = c * T
        w = np.zeros((TH, D), f32)
        s0 = max(0, lo - 3)
        w[3 - (lo - s0):] = x[s0:lo + T]
        asl = alog[c * 512:(c + 1) * 512].reshape(4, 128, S)
        m = dict(com)
        m["xfm"] = np.ascontiguousarray(w.T)
        m["w_dts"] = np.ascontiguousarray(dtw[c * 512:(c + 1) * 512].T)
        m["dtbs"] = np.ascontiguousarray(
            dtb[c * 512:(c + 1) * 512].reshape(4, 128).T)
        m["alog"] = np.ascontiguousarray(
            asl.transpose(1, 0, 2).reshape(128, 4 * S))
        m["m3"] = np.full((128, 1), 0.0 if c == 0 else 1.0, f32)
        m["coff"] = np.array([[512 * c]], np.int32)
        m["coff2"] = np.array(
            [[(c // 2) * 8192 + (512 * c) % 1024]], np.int32)
        in_maps.append(m)
    return in_maps


def kernel(**inputs):
    nc = build(upto=9)
    in_maps = stage(inputs)
    res = run_bass_kernel_spmd(nc, in_maps, core_ids=list(range(NC)))
    outs = [np.asarray(res.results[c]["out"]).T for c in range(NC)]
    return np.concatenate(outs, axis=0).reshape(1, L, D).astype(np.float32)


# version 06
# speedup vs baseline: 1.0129x; 1.0129x over previous
import numpy as np
import ml_dtypes
from contextlib import ExitStack
import concourse.bass as bass
import concourse.mybir as mybir
from concourse.alu_op_type import AluOpType as OP
from concourse.bass_utils import run_bass_kernel_spmd

F32 = mybir.dt.float32
BF16 = mybir.dt.bfloat16
I32 = mybir.dt.int32
AF = mybir.ActivationFunctionType
AX = mybir.AxisListType
ET = mybir.EngineType
RG = [[0, 1, 2, 3, 4, 5, 6, 7]]

NC, L, T, TH = 8, 16384, 2048, 2051
D, E, S, RK, H = 2048, 4096, 16, 128, 8192
NT_D, NT_E, NT_H = 16, 32, 64
EPS = 1e-5
SPCL = 30.0
TH_CH = [(0, 512), (512, 512), (1024, 512), (1536, 512), (2048, 3)]
T_CH = [(0, 512), (512, 512), (1024, 512), (1536, 512)]
ARENA = 48128


class Plan:
    ENGINES = ("sync", "gpsimd", "vector", "scalar", "tensor")

    def __init__(self, nc, st):
        self.nc, self.st = nc, st
        self.q = {e: [] for e in self.ENGINES}
        self.sems, self.cnt = {}, {}

    def sem(self, name):
        if name not in self.sems:
            self.sems[name] = self.st.enter_context(self.nc.semaphore(name))
            self.cnt[name] = 0
        return self.sems[name]

    def now(self, name):
        self.sem(name)
        return self.cnt[name]

    def step(self, eng, fn, waits=(), inc=None, inc_k=1):
        waits = [(self.sem(n), v) for (n, v) in waits if v > 0]
        sem = self.sem(inc) if inc else None
        if inc:
            self.cnt[inc] += inc_k

        def run(engine, waits=waits, fn=fn, sem=sem, inc_k=inc_k):
            for s, v in waits:
                engine.wait_ge(s, v)
            instr = fn(engine)
            if sem is not None:
                instr.then_inc(sem, inc_k)
        self.q[eng].append(run)
        return self.cnt[inc] if inc else None

    def dma(self, eng, out, in_, waits=(), track="dS", selfwait=True, **kw):
        if selfwait:
            waits = list(waits) + [(track, self.now(track))]
        v = self.step(eng, lambda e: e.dma_start(out=out, in_=in_, **kw),
                      waits=waits, inc=track, inc_k=16)
        return (track, v)

    def dma_dyn(self, eng, out, in_fn, waits=(), track="dS", selfwait=True,
                **kw):
        if selfwait:
            waits = list(waits) + [(track, self.now(track))]
        v = self.step(eng, lambda e: e.dma_start(out=out, in_=in_fn(), **kw),
                      waits=waits, inc=track, inc_k=16)
        return (track, v)

    def fence(self):
        vals = [(n, self.cnt[n]) for n in self.cnt]
        sems = self.sems
        for name in self.ENGINES:
            def run(engine, vals=vals):
                for n, v in vals:
                    if v > 0:
                        engine.wait_ge(sems[n], v)
            self.q[name].append(run)

    def emit(self, block):
        for name in self.ENGINES:
            steps = self.q[name]

            def body(engine, steps=steps):
                for s in steps:
                    s(engine)
            getattr(block, name)(body)


class Arena:
    def __init__(self, t):
        self.t, self.top, self.marks, self.peak = t, 0, [], 0

    def f32(self, cols):
        a = self.top
        self.top += cols
        self.peak = max(self.peak, self.top)
        assert self.top <= ARENA, f"arena overflow {self.top}"
        return self.t[:, a:a + cols]

    def bf16(self, n):
        assert n % 2 == 0
        a = self.top
        self.top += n // 2
        self.peak = max(self.peak, self.top)
        assert self.top <= ARENA, f"arena overflow {self.top}"
        return self.t[:, a:a + n // 2].bitcast(BF16)

    def push(self):
        self.marks.append(self.top)

    def pop(self):
        self.top = self.marks.pop()


class _Stop(Exception):
    pass


def build(upto=9):
    nc = bass.Bass("TRN2", target_bir_lowering=False, debug=False)
    dram = {}

    def din(name, shape, dt=F32):
        dram[name] = nc.dram_tensor(name, shape, dt, kind="ExternalInput").ap()
        return dram[name]

    def dint(name, shape, dt=F32):
        dram[name] = nc.dram_tensor(name, shape, dt, kind="Internal").ap()
        return dram[name]

    xfm = din("xfm", [D, TH])
    w_ip = din("w_ip", [NT_H, D, 128])
    w_xp = din("w_xp", [2, E, 128])
    w_dts = din("w_dts", [128, 512])
    w_op = din("w_op", [NT_D, E, 128])
    w_fp = din("w_fp", [NT_H, D, 128])
    w_fo = din("w_fo", [NT_D, E, 128])
    w_fnt = din("w_fnt", [NT_D, D, 128])
    w_fnr = din("w_fnr", [D, D])
    cvw = din("cvw", [E, 4])
    for nm in ("cvb", "dsk"):
        din(nm, [128, NT_E])
    din("dtbs", [128, 4])
    din("alog", [128, 4 * S])
    for nm in ("ln1w", "ln1b", "ln2w", "ln2b", "fnb"):
        din(nm, [128, NT_D])
    din("ident", [128, 128])
    din("identu", [128, 128], mybir.dt.uint8)
    din("onecol", [128, 1], BF16)
    din("onefull", [128, 128], BF16)
    din("m3", [128, 1])
    din("zcol", [128, 64])
    din("zb", [128, 128], BF16)
    din("cvals", [128, 4])
    din("coff", [1, 1], I32)
    din("coff2", [1, 1], I32)

    out = nc.dram_tensor("out", [D, T], F32, kind="ExternalOutput").ap()
    dbg = nc.dram_tensor("dbg", [256, T], F32, kind="ExternalOutput").ap()

    EP = E + 128
    sz_sp = dint("sz_sp", [E, T])
    xcf_sp = dint("xcf_sp", [E, T])
    y_sp = dint("y_sp", [E, T], BF16)
    hg_sp = dint("hg_sp", [E, T], BF16)
    t_sp = dint("t_sp", [D, T])
    x1_sp = dint("x1_sp", [D, T])
    ab_sp = dint("ab_sp", [D, T], BF16)
    o_sp = dint("o_sp", [D, T], BF16)
    agi = dint("agi", [EP, T], BF16)
    agoq = dint("agoq", [4 * NC * 1024, T], BF16)
    agod = dint("agod", [NC * 128, T], BF16)
    bci = dint("bci", [2 * S, T], BF16)
    bco = dint("bco", [NC * 2 * S, T], BF16)
    ygi = dint("ygi", [E, T], BF16)
    ygo = dint("ygo", [NC * E, T], BF16)
    al_dr = dint("al_dr", [NT_D, 128])
    myag = dint("myag", [NC * 512, T], BF16)
    myy = dint("myy", [E, T], BF16)
    row_dr = dint("row_dr", [4, TH])
    st_dr = dint("st_dr", [2 * 128, TH])

    st = ExitStack()
    with st:
        P = Plan(nc, st)

        def sb(name, shape, dt):
            return st.enter_context(nc.sbuf_tensor(name, shape, dt))

        s_par = {}
        for nm, w in [("cvb", NT_E), ("dtbs", 4), ("dsk", NT_E), ("ln1w", NT_D),
                      ("ln1b", NT_D), ("ln2w", NT_D), ("ln2b", NT_D), ("fnb", NT_D)]:
            s_par[nm] = sb("s_" + nm, [128, w], F32)
        s_acol = sb("s_acol", [128, 4 * S], F32)
        s_ident = sb("s_ident", [128, 128], F32)
        s_identu = sb("s_identu", [128, 128], mybir.dt.uint8)
        s_one = sb("s_one", [128, 1], BF16)
        s_onef = sb("s_onef", [128, 128], BF16)
        s_m3 = sb("s_m3", [128, 1], F32)
        s_cvw = sb("s_cvw", [128, NT_E * 4], F32)
        s_hst = sb("s_hst", [128, 64], F32)
        s_coff = sb("s_coff", [1, 1], I32)
        s_coff2 = sb("s_coff2", [1, 1], I32)
        s_zb = sb("s_zb", [128, 128], BF16)
        s_cvals = sb("s_cvals", [128, 4], F32)
        s_dtws = sb("s_dtws", [128, 512], BF16)
        s_identb = sb("s_identb", [128, 128], BF16)

        s_alpha = sb("s_alpha", [128, NT_D], F32)
        s_arow = sb("s_arow", [1, D], F32)
        s_a16 = sb("s_a16", [NT_D, 128], F32)
        arena_t = sb("arena", [128, ARENA], F32)
        A = Arena(arena_t)
        hold = {}

        pb = [st.enter_context(nc.psum_tensor(f"pb{i}", [128, 512], F32))
              for i in range(8)]

        def stop_dump(a_ap=None, b_ap=None):
            w = [(n, P.cnt[n]) for n in P.cnt]
            if a_ap is not None:
                P.dma("gpsimd", dbg[0:a_ap.shape[0], 0:a_ap.shape[1]], a_ap,
                      waits=w, track="ddbg")
            if b_ap is not None:
                P.dma("gpsimd", dbg[128:128 + b_ap.shape[0], 0:b_ap.shape[1]],
                      b_ap, waits=w, track="ddbg")
            P.fence()
            raise _Stop

        def col(par, j):
            return s_par[par][:, j:j + 1]

        try:
            for nm in ("cvb", "dtbs", "dsk", "ln1w", "ln1b", "ln2w", "ln2b", "fnb"):
                P.dma("sync", s_par[nm][:], dram[nm], track="d0")
            P.dma("sync", s_ident[:], dram["ident"], track="d0")
            P.dma("sync", s_identu[:], dram["identu"], track="d0")
            P.dma("sync", s_m3[:], dram["m3"], track="d0")
            P.dma("sync", s_one[:], dram["onecol"], track="d0")
            P.dma("sync", s_onef[:], dram["onefull"], track="d0")
            P.dma("sync", s_hst[:], dram["zcol"], track="d0")
            P.dma("sync", s_coff[:], dram["coff"], track="d0")
            P.dma("sync", s_coff2[:], dram["coff2"], track="d0")
            P.dma("sync", s_zb[:], dram["zb"], track="d0")
            P.dma("sync", s_cvals[:], dram["cvals"], track="d0")
            nc.const_aps.register(F32, EPS, s_cvals[:, 0:1])
            nc.const_aps.register(F32, 1.0, s_cvals[:, 1:2])
            nc.const_aps.register(F32, 0.0, s_cvals[:, 2:3])
            P.dma("sync", s_cvw[:].rearrange("p (j k) -> p j k", k=4),
                  cvw.rearrange("(j p) k -> p j k", p=128), track="d0")
            P.dma("sync", s_acol[:], dram["alog"], track="d0")
            P.dma("gpsimd", s_dtws[:], dram["w_dts"], track="d0g")
            P.dma("gpsimd", s_identb[:], dram["ident"], track="d0g")
            vs = P.now("d0")
            va0 = P.step("scalar", lambda e: e.activation(
                out=s_acol[:], in_=s_acol[:], func=AF.Exp),
                waits=[("d0", vs)], inc="cA")
            P.step("scalar", lambda e: e.activation(
                out=s_acol[:], in_=s_acol[:], func=AF.Copy, scale=-1.0),
                waits=[("cA", va0)], inc="cA")
            vcoff = vs

            def _load_off(e):
                hold["off"] = nc.values_load(s_coff[:], engines=[ET.SP],
                                             min_val=0, max_val=3584,
                                             skip_runtime_bounds_check=True)
                hold["off2"] = nc.values_load(s_coff2[:], engines=[ET.SP],
                                              min_val=0, max_val=25088,
                                              skip_runtime_bounds_check=True)
                return None
            P.q["sync"].append(lambda e: (e.wait_ge(P.sems["d0"], vcoff),
                                          _load_off(e)) and None)

            A.push()
            wtmp = A.f32(2 * D)
            s_wt = [wtmp[:, 0:D], wtmp[:, D:2 * D]]
            redv = [0] * NT_D
            for j in range(NT_D):
                w = [("cV", redv[j - 2])] if j >= 2 else []
                v = P.dma("sync", s_wt[j % 2], w_fnr[j * 128:(j + 1) * 128, :],
                          waits=w, track=f"wA{j % 2}")
                redv[j] = P.step("vector", lambda e, j=j: e.tensor_reduce(
                    out=s_alpha[:, j:j + 1], in_=s_wt[j % 2],
                    axis=AX.X, op=OP.add, apply_absolute_value=True),
                    waits=[v], inc="cV")
            P.step("vector", lambda e: e.tensor_scalar(
                out=s_alpha[:], in0=s_alpha[:], scalar1=1.0 / D, scalar2=None,
                op0=OP.mult), waits=[("cV", redv[NT_D - 1])], inc="cV")
            P.step("tensor", lambda e: e.transpose(pb[0][0:NT_D, 0:128], s_alpha[:],
                                                   s_ident[:]),
                   waits=[("cV", P.now("cV")), ("d0", P.now("d0"))], inc="cT")
            P.step("scalar", lambda e: e.activation(out=s_a16[:], in_=pb[0][0:NT_D, 0:128],
                                                    func=AF.Copy),
                   waits=[("cT", P.now("cT"))], inc="cA")
            va16 = P.dma("gpsimd", al_dr, s_a16[:],
                         waits=[("cA", P.now("cA"))], track="dal")
            P.dma("gpsimd", s_arow[:],
                  al_dr.rearrange("a b -> (a b)").unsqueeze(0),
                  waits=[va16], track="dal2")
            A.pop()
            P.fence()

            def stats_resident(tile_of, ntiles, grid, sqb, sum_t, sq_t, ready):
                for (c0, cw) in grid:
                    sqmm = [0] * ntiles
                    for j in range(ntiles):
                        P.step("tensor", lambda e, j=j, c0=c0, cw=cw: e.matmul(
                            pb[6][:, 0:cw], s_onef[:], tile_of(j)[:, c0:c0 + cw],
                            start=(j == 0), stop=(j == ntiles - 1)),
                            waits=[("cA", ready[j])],
                            inc="cT" if j == ntiles - 1 else None)
                        vq = P.step("scalar", lambda e, j=j, c0=c0, cw=cw: e.activation(
                            out=sqb[j % 2][:, 0:cw], in_=tile_of(j)[:, c0:c0 + cw],
                            func=AF.Square),
                            waits=[("cT", sqmm[j - 2])] if j >= 2 else [], inc="cA")
                        sqmm[j] = P.step("tensor", lambda e, j=j, cw=cw: e.matmul(
                            pb[7][:, 0:cw], s_onef[:], sqb[j % 2][:, 0:cw],
                            start=(j == 0), stop=(j == ntiles - 1)),
                            waits=[("cA", vq)], inc="cT")
                    vt = P.now("cT")
                    P.step("scalar", lambda e, c0=c0, cw=cw: e.activation(
                        out=sum_t[:, c0:c0 + cw], in_=pb[6][:, 0:cw], func=AF.Copy),
                        waits=[("cT", vt)], inc="cA")
                    va = P.step("scalar", lambda e, c0=c0, cw=cw: e.activation(
                        out=sq_t[:, c0:c0 + cw], in_=pb[7][:, 0:cw], func=AF.Copy),
                        inc="cA")
                    P.q["tensor"].append(
                        lambda e, va=va: e.wait_ge(P.sems["cA"], va))
                return P.now("cA")

            def stats_streamed(load_fn, sqb, bufs, sum_t, sq_t):
                lastmm = [0] * NT_D
                for j in range(NT_D):
                    b = bufs[j % 2]
                    v = load_fn(j, b, [("cT", lastmm[j - 2])] if j >= 2 else [])
                    vq = P.step("scalar", lambda e, b=b, j=j: e.activation(
                        out=sqb[j % 2][:, 0:T], in_=b[:, 0:T], func=AF.Square),
                        waits=[v] + ([("cT", lastmm[j - 2])] if j >= 2 else []),
                        inc="cA")
                    for ci, (c0, cw) in enumerate(T_CH):
                        P.step("tensor", lambda e, b=b, c0=c0, cw=cw, ci=ci, j=j: e.matmul(
                            pb[ci][:, 0:cw], s_onef[:], b[:, c0:c0 + cw],
                            start=(j == 0), stop=(j == NT_D - 1)),
                            waits=[v] if ci == 0 else [],
                            inc=None)
                        lastmm[j] = P.step("tensor",
                            lambda e, c0=c0, cw=cw, ci=ci, j=j: e.matmul(
                            pb[4 + ci][:, 0:cw], s_onef[:], sqb[j % 2][:, c0:c0 + cw],
                            start=(j == 0), stop=(j == NT_D - 1)),
                            waits=[("cA", vq)] if ci == 0 else [],
                            inc="cT")
                vt = P.now("cT")
                for ci, (c0, cw) in enumerate(T_CH):
                    P.step("scalar", lambda e, c0=c0, cw=cw, ci=ci: e.activation(
                        out=sum_t[:, c0:c0 + cw], in_=pb[ci][:, 0:cw], func=AF.Copy),
                        waits=[("cT", vt)] if ci == 0 else [], inc="cA")
                    P.step("scalar", lambda e, c0=c0, cw=cw, ci=ci: e.activation(
                        out=sq_t[:, c0:c0 + cw], in_=pb[4 + ci][:, 0:cw],
                        func=AF.Copy), inc="cA")
                return P.now("cA")

            def ln_math(stats_done, sum_t, sq_t, m_t, i_t, W, denom,
                        chain=False):
                va = P.step("scalar", lambda e: e.activation(
                    out=m_t[:, 0:W], in_=sum_t[:, 0:W], func=AF.Copy,
                    scale=1.0 / denom), waits=[("cA", stats_done)], inc="cA")
                va = P.step("scalar", lambda e: e.activation(
                    out=sq_t[:, 0:W], in_=sq_t[:, 0:W], func=AF.Copy,
                    scale=1.0 / denom), waits=[("cA", va)], inc="cA")
                vv = P.step("vector", lambda e: e.tensor_tensor(
                    out=sum_t[:, 0:W], in0=m_t[:, 0:W], in1=m_t[:, 0:W],
                    op=OP.mult), waits=[("cA", va)], inc="cV")
                vv = P.step("vector", lambda e: e.tensor_tensor(
                    out=sq_t[:, 0:W], in0=sq_t[:, 0:W], in1=sum_t[:, 0:W],
                    op=OP.subtract), waits=[("cV", vv)], inc="cV")
                va = P.step("scalar", lambda e: e.activation(
                    out=sum_t[:, 0:W], in_=sq_t[:, 0:W], func=AF.Sqrt,
                    bias=EPS), waits=[("cV", vv)], inc="cA")
                vv = P.step("vector", lambda e: e.reciprocal(
                    out=i_t[:, 0:W], in_=sum_t[:, 0:W]),
                    waits=[("cA", va)], inc="cV")
                if not chain:
                    return vv
                vv = P.step("vector", lambda e: e.tensor_tensor(
                    out=sum_t[:, 0:W], in0=i_t[:, 0:W], in1=i_t[:, 0:W],
                    op=OP.mult), waits=[("cV", vv)], inc="cV")
                vv = P.step("vector", lambda e: e.tensor_tensor(
                    out=sum_t[:, 0:W], in0=sq_t[:, 0:W], in1=sum_t[:, 0:W],
                    op=OP.mult), waits=[("cV", vv)], inc="cV")
                va = P.step("scalar", lambda e: e.activation(
                    out=sum_t[:, 0:W], in_=sum_t[:, 0:W], func=AF.Sqrt,
                    bias=EPS), waits=[("cV", vv)], inc="cA")
                vv = P.step("vector", lambda e: e.reciprocal(
                    out=sum_t[:, 0:W], in_=sum_t[:, 0:W]),
                    waits=[("cA", va)], inc="cV")
                vv = P.step("vector", lambda e: e.tensor_copy(
                    sq_t[:, 0:W], i_t[:, 0:W]), waits=[("cV", vv)], inc="cV")
                return P.step("vector", lambda e: e.tensor_tensor(
                    out=i_t[:, 0:W], in0=i_t[:, 0:W], in1=sum_t[:, 0:W],
                    op=OP.mult), waits=[("cV", vv)], inc="cV")

            def slab_load(dst, wsrc, j, waits=(), track="dG"):
                return P.dma("gpsimd", dst.rearrange("p (kt m) -> p kt m", m=128),
                             wsrc[j].rearrange("(kt p) m -> p kt m", p=128),
                             waits=waits, track=track)

            A.push()
            xb_all = A.bf16(NT_D * (TH + 1))

            def xb_t(j):
                return xb_all[:, j * (TH + 1):j * (TH + 1) + TH]

            A.push()
            xf = [A.f32(TH), A.f32(TH)]
            sqb = [A.bf16(512), A.bf16(512)]
            castv = [0] * NT_D
            for j in range(NT_D):
                w = [("cA", castv[j - 2])] if j >= 2 else []
                v = P.dma("sync", xf[j % 2], xfm[j * 128:(j + 1) * 128, :],
                          waits=w, track=f"xf{j % 2}")
                castv[j] = P.step("scalar", lambda e, j=j: e.activation(
                    out=xb_t(j), in_=xf[j % 2], func=AF.Copy),
                    waits=[v], inc="cA")
            sum_t = xf[0]
            sq_t = xf[1]
            repm = A.f32(TH)
            repi = A.f32(TH)
            sdone = stats_resident(xb_t, NT_D, TH_CH, sqb, sum_t, sq_t, castv)
            vr2 = ln_math(sdone, sum_t, sq_t, repm, repi, TH, D, chain=True)
            vsp1 = P.dma("sync", st_dr[0:128, :], repm[:, 0:TH],
                         waits=[("cV", vr2)], track="dst1")
            vsp2 = P.dma("sync", st_dr[128:256, :], sq_t[:, 0:TH],
                         waits=[("cV", vr2)], track="dst1")
            gb = [A.bf16(TH + 1), A.bf16(TH + 1)]
            u_in = A.bf16(NT_D * (TH + 1))

            def u_t(j):
                return u_in[:, j * (TH + 1):j * (TH + 1) + TH]

            uv = [0] * NT_D
            for j in range(NT_D):
                w = [("cV", vr2)] + ([("cV", uv[j - 2])] if j >= 2 else [])
                vv = P.step("vector", lambda e, j=j: e.tensor_tensor(
                    out=gb[j % 2][:, 0:TH], in0=xb_t(j), in1=repm[:, 0:TH],
                    op=OP.subtract), waits=w, inc="cV")
                vv = P.step("vector", lambda e, j=j: e.scalar_tensor_tensor(
                    out=u_t(j), in0=gb[j % 2][:, 0:TH], scalar=col("ln1w", j),
                    in1=repi[:, 0:TH], op0=OP.mult, op1=OP.mult),
                    waits=[("cV", vv)], inc="cV")
                vv = P.step("vector", lambda e, j=j: e.tensor_scalar(
                    out=u_t(j), in0=u_t(j), scalar1=col("ln1b", j), scalar2=None,
                    op0=OP.add), waits=[("cV", vv)], inc="cV")
                uv[j] = P.step("vector", lambda e, j=j: e.tensor_scalar(
                    out=u_t(j)[:, 0:3], in0=u_t(j)[:, 0:3], scalar1=s_m3[:],
                    scalar2=None, op0=OP.mult), waits=[("cV", vv)], inc="cV")
            P.fence()
            A.pop()
            scratch0 = A.top
            _resv = A.f32(2 * TH + 512 + 2 * TH + (TH + 1))
            _u_in_resv = A.bf16(NT_D * (TH + 1))

            class Sub:
                def __init__(self, base, cols):
                    self.a, self.n, self.cols = base, 0, cols

                def f32(self, c):
                    v = self.a[:, self.n:self.n + c]
                    self.n += c
                    assert self.n <= self.cols
                    return v

                def bf16(self, c):
                    assert c % 2 == 0
                    v = self.a[:, self.n:self.n + c // 2].bitcast(BF16)
                    self.n += c // 2
                    assert self.n <= self.cols
                    return v

                def reset(self):
                    self.n = 0
            SA = Sub(arena_t[:, scratch0 - 0:scratch0], 0)
            SA = Sub(arena_t[:, 0:scratch0], scratch0)

            if upto <= 1:
                stop_dump(u_t(0)[:, 0:2048], u_t(15)[:, 0:2048])
            slab = [SA.bf16(2048), SA.bf16(2048)]
            xmb = [SA.bf16(TH + 1), SA.bf16(TH + 1)]
            diag = [SA.bf16(512), SA.bf16(512)]
            xcf = [SA.f32(512), SA.f32(512)]
            xcb = [SA.bf16(512), SA.bf16(512)]
            szb = [SA.f32(512), SA.f32(512)]

            slabv = [("dG", 0), ("dG", 0)]
            slab_used = [0, 0]
            xm_used = [0, 0]
            diag_used = [0, 0]
            xcf_used = [("xcf0", 0), ("xcf1", 0)]
            xcb_used = [("xcb0", 0), ("xcb1", 0)]
            sz_used = [("sz0", 0), ("sz1", 0)]
            bank_ev = [0] * 8
            for j in range(NT_H):
                if j in (8, 16, 24, 32):
                    qq = j // 8 - 1
                    P.step("gpsimd", lambda e, qq=qq: e.collective_compute(
                        "AllGather", OP.bypass, replica_groups=RG,
                        ins=[agi[qq * 1024:(qq + 1) * 1024]],
                        outs=[agoq[qq * NC * 1024:(qq + 1) * NC * 1024]]),
                        waits=[("xcb0", P.now("xcb0")),
                               ("xcb1", P.now("xcb1"))],
                        inc="cC")
                db = j % 2
                slabv[db] = slab_load(slab[db], w_ip, j,
                                      waits=[("cT", slab_used[db])],
                                      track=f"sl{db}")
                grid = TH_CH if j < 32 else T_CH
                off = 0 if j < 32 else 3
                for ci, (c0, cw) in enumerate(grid):
                    bk = (j * len(grid) + ci) % 2
                    for k in range(NT_D):
                        w = []
                        if k == 0:
                            w = [slabv[db], ("cA", bank_ev[bk])]
                        P.step("tensor", lambda e, db=db, k=k, c0=c0, cw=cw, bk=bk,
                               off=off: e.matmul(
                                   pb[bk][:, 0:cw], slab[db][:, k * 128:(k + 1) * 128],
                                   u_t(k)[:, off + c0:off + c0 + cw],
                                   start=(k == 0), stop=(k == NT_D - 1)),
                               waits=w, inc="cT" if k == NT_D - 1 else None)
                    vt = P.now("cT")
                    if ci == len(grid) - 1:
                        slab_used[db] = vt
                    if j < 32:
                        bank_ev[bk] = P.step(
                            "scalar", lambda e, db=db, c0=c0, cw=cw, bk=bk:
                            e.activation(out=xmb[db][:, c0:c0 + cw],
                                         in_=pb[bk][:, 0:cw], func=AF.Copy),
                            waits=[("cT", vt)] +
                            ([("cT", xm_used[db])] if ci == 0 else []),
                            inc="cA")
                    else:
                        va = P.step("scalar", lambda e, ci=ci, c0=c0, cw=cw, bk=bk:
                                    e.activation(out=szb[ci % 2][:, 0:cw],
                                                 in_=pb[bk][:, 0:cw], func=AF.Silu),
                                    waits=[("cT", vt), sz_used[ci % 2]],
                                    inc="cA")
                        bank_ev[bk] = va
                        sz_used[ci % 2] = P.dma(
                            "sync", sz_sp[(j - 32) * 128:(j - 31) * 128, c0:c0 + cw],
                            szb[ci % 2][:, 0:cw], waits=[("cA", va)],
                            track=f"sz{ci % 2}")
                if j < 32:
                    vd = 0
                    for kk in range(4):
                        vd = P.step("vector", lambda e, j=j, db=db, kk=kk:
                                    e.select(
                            out=diag[db][:, kk * 128:(kk + 1) * 128],
                            mask=s_identu[:],
                            on_true=s_cvw[:, j * 4 + kk:j * 4 + kk + 1]
                            .broadcast_to([128, 128]),
                            on_false=s_zb[:], add_drain=True),
                            waits=([("cT", diag_used[db])] if kk == 0 else
                                   [("cV", vd)]),
                            inc="cV")
                    xm_ready = P.now("cA")
                    for ci, (c0, cw) in enumerate(T_CH):
                        bk = 2 + (j * 4 + ci) % 2
                        for k in range(4):
                            w = []
                            if k == 0:
                                w = [("cV", vd), ("cA", bank_ev[bk]),
                                     ("cA", xm_ready)]
                            P.step("tensor", lambda e, db=db, k=k, c0=c0, cw=cw,
                                   bk=bk: e.matmul(
                                       pb[bk][:, 0:cw],
                                       diag[db][:, k * 128:(k + 1) * 128],
                                       xmb[db][:, c0 + k:c0 + k + cw],
                                       start=(k == 0), stop=(k == 3)),
                                   waits=w, inc="cT" if k == 3 else None)
                        vt = P.now("cT")
                        if ci == 3:
                            xm_used[db] = vt
                            diag_used[db] = vt
                        va = P.step("scalar", lambda e, j=j, ci=ci, cw=cw, bk=bk:
                                    e.activation(out=xcf[ci % 2][:, 0:cw],
                                                 in_=pb[bk][:, 0:cw], func=AF.Silu,
                                                 bias=col("cvb", j)),
                                    waits=[("cT", vt), xcf_used[ci % 2]],
                                    inc="cA")
                        bank_ev[bk] = va
                        va2 = P.step("scalar", lambda e, ci=ci, cw=cw: e.activation(
                            out=xcb[ci % 2][:, 0:cw], in_=xcf[ci % 2][:, 0:cw],
                            func=AF.Copy),
                            waits=[xcb_used[ci % 2], ("cA", va)],
                            inc="cA")
                        xcf_used[ci % 2] = P.dma(
                            "sync", xcf_sp[j * 128:(j + 1) * 128, c0:c0 + cw],
                            xcf[ci % 2][:, 0:cw], waits=[("cA", va)],
                            track=f"xcf{ci % 2}")
                        xcb_used[ci % 2] = P.dma(
                            "sync", agi[j * 128:(j + 1) * 128, c0:c0 + cw],
                            xcb[ci % 2][:, 0:cw], waits=[("cA", va2)],
                            track=f"xcb{ci % 2}")
            P.fence()

            if upto == 24:
                stop_dump(xcf_sp[31 * 128:32 * 128, :], agi[17 * 128:18 * 128, :])
            if upto <= 2:
                stop_dump(xcf_sp[0:128, :], sz_sp[0:128, :])
            SA = Sub(arena_t[:, 0:ARENA], ARENA)
            xslab = [SA.bf16(4096), SA.bf16(4096)]
            xcs = [SA.bf16(512), SA.bf16(512)]
            dtin = SA.bf16(2048)
            bcB = SA.bf16(2048)

            v0 = slab_load(xslab[0], w_xp, 0, track="dxp")
            v1 = slab_load(xslab[1], w_xp, 1, track="dxp")
            vxp = ("dxp", v1[1])
            xpe = [0, 0]
            xcs_guard = [0, 0]
            for ci, (c0, cw) in enumerate(T_CH):
                lmm = [0] * NT_E
                for k in range(NT_E):
                    vld = P.dma("sync", xcs[k % 2][:, 0:cw],
                                agi[k * 128:(k + 1) * 128, c0:c0 + cw],
                                waits=[("cT", xcs_guard[k % 2])],
                                track=f"xcs{k % 2}")
                    for m in range(2):
                        w = [vld]
                        if k == 0:
                            w += [vxp, ("cA", xpe[m])]
                        lmm[k] = P.step("tensor", lambda e, k=k, m=m, cw=cw: e.matmul(
                            pb[m][:, 0:cw], xslab[m][:, k * 128:(k + 1) * 128],
                            xcs[k % 2][:, 0:cw],
                            start=(k == 0), stop=(k == NT_E - 1)),
                            waits=w, inc="cT")
                    xcs_guard[k % 2] = lmm[k]
                vt = P.now("cT")
                xpe[0] = P.step("scalar", lambda e, c0=c0, cw=cw: e.activation(
                    out=dtin[:, c0:c0 + cw], in_=pb[0][:, 0:cw], func=AF.Copy),
                    waits=[("cT", vt)], inc="cA")
                xpe[1] = P.step("scalar", lambda e, c0=c0, cw=cw: e.activation(
                    out=bcB[0:32, c0:c0 + cw], in_=pb[1][0:32, 0:cw], func=AF.Copy),
                    inc="cA")
            vbc = P.dma("sync", bci[:, :], bcB[0:32, 0:T],
                        waits=[("cA", P.now("cA"))], track="dbc")

            if upto == 21:
                stop_dump(dtin[:, 0:T], bcB[0:32, 0:T])
            vdst = P.dma("sync", agi[E:E + 128, :], dtin[:, 0:T],
                         waits=[("cA", P.now("cA"))], track="agd0")
            if upto == 25:
                stop_dump(agi[0:128, :], agi[E:E + 128, :])
            P.step("gpsimd", lambda e: e.collective_compute(
                "AllGather", OP.bypass, replica_groups=RG,
                ins=[agi[E:E + 128]], outs=[agod]),
                waits=[vdst], inc="cC")
            P.step("gpsimd", lambda e: e.collective_compute(
                "AllGather", OP.bypass, replica_groups=RG, ins=[bci], outs=[bco]),
                waits=[vbc], inc="cC")
            vcoll = P.now("cC")
            P.dma_dyn("sync", bass.AP(myag.tensor, 0,
                                      [[512 * T, NC], [T, 512], [1, T]]),
                      lambda: bass.AP(agoq.tensor, hold["off2"] * T,
                                      [[1024 * T, NC], [T, 512], [1, T]]),
                      waits=[("cC", 4)], track="dmy")
            vgat = ("dmy", P.now("dmy"))
            P.fence()

            if upto <= 3:
                stop_dump(agi[0:128, :], bci[0:32, :])
            SA.reset()
            repBall = SA.bf16(S * 2048)
            repB = [repBall[:, s * 2048:(s + 1) * 2048] for s in range(S)]
            repCall = SA.bf16(S * 2048)
            repC = [repCall[:, s * 2048:(s + 1) * 2048] for s in range(S)]
            dtt = [SA.bf16(2048), SA.bf16(2048)]
            wct = [SA.bf16(2048), SA.bf16(2048)]
            dinb = SA.bf16(2048)
            dAb = [SA.f32(T), SA.f32(T)]
            bbf = SA.bf16(2048)
            hhb = [SA.bf16(2048), SA.bf16(2048)]
            gcb = SA.bf16(2048)
            ybo = SA.bf16(2048)
            sp_t = SA.f32(512)
            sp_e = SA.bf16(512)

            rB_used = [0] * S
            rC_used = [0] * S
            vB = [0] * S
            vC = [0] * S
            dtt_guard = [0, 0]
            wct_used = [0, 0]
            evac_tok = [0, 0]
            hcopy = [0] * 64
            h_used = [0, 0]
            dA_slot = [0, 0]
            tts_prev = 0
            gc_mm_last = 0
            dtmm_last = [0]
            spe_guard = [0]
            ln_last = [0]
            vln_j = [0, 0]
            vts_j = [0, 0]
            dtmm_j = [0, 0]
            vdl_h = [0]
            ybo_guard = ("ygs", 0)
            for cc in range(NC):
                def emit_dt(j, cc=cc):
                    jsl = j % 2
                    grp = 4 * jsl
                    for q in range(4):
                        w = []
                        if q == 0:
                            w = [vdl_h[0], ("d0g", 32), ("cA", evac_tok[jsl])]
                        vm = P.step(
                            "tensor", lambda e, j=j, q=q, grp=grp: e.matmul(
                                pb[grp + q][:, 0:512],
                                s_dtws[:, j * 128:(j + 1) * 128],
                                dinb[:, q * 512:(q + 1) * 512],
                                start=True, stop=True),
                            waits=w, inc="cT")
                        dtmm_j[jsl] = vm
                        dtmm_last[0] = vm
                        vts = P.step("vector", lambda e, j=j, q=q, grp=grp:
                                     e.tensor_scalar(
                                         out=sp_t[:, 0:512],
                                         in0=pb[grp + q][:, 0:512],
                                         scalar1=col("dtbs", j), scalar2=SPCL,
                                         op0=OP.add, op1=OP.min),
                                     waits=[("cT", vm),
                                            ("cA", spe_guard[0])], inc="cV")
                        vts_j[jsl] = vts
                        vex = P.step("scalar", lambda e: e.activation(
                            out=sp_e[:, 0:512], in_=sp_t[:, 0:512], func=AF.Exp),
                            waits=[("cV", vts), ("cA", ln_last[0])], inc="cA")
                        spe_guard[0] = vex
                        vln = P.step(
                            "scalar", lambda e, jsl=jsl, q=q: e.activation(
                                out=dtt[jsl][:, q * 512:(q + 1) * 512],
                                in_=sp_e[:, 0:512], func=AF.Ln, bias=1.0),
                            waits=[("cA", vex)] +
                                  ([("cA", dtt_guard[jsl])] if q == 0 else []),
                            inc="cA")
                        vln_j[jsl] = vln
                        ln_last[0] = vln

                vdl_h[0] = P.dma("sync", dinb[:, 0:T],
                                 agod[cc * 128:(cc + 1) * 128, :],
                                 waits=[("cC", vcoll), ("cT", dtmm_last[0])],
                                 track="dtl")
                for sp in range(0, S, 2):
                    v = P.dma("sync",
                              repBall[:, sp * 2048:(sp + 2) * 2048]
                              .rearrange("p (i t) -> p i t", i=2),
                              bco[cc * 2 * S + sp:cc * 2 * S + sp + 2, :]
                              .unsqueeze(0).broadcast_to([128, 2, T]),
                              waits=[("cC", vcoll), ("cV", rB_used[sp]),
                                     ("cV", rB_used[sp + 1])],
                              track=f"rB{(sp // 2) % 4}")
                    vB[sp] = vB[sp + 1] = v
                    v = P.dma("gpsimd",
                              repCall[:, sp * 2048:(sp + 2) * 2048]
                              .rearrange("p (i t) -> p i t", i=2),
                              bco[cc * 2 * S + S + sp:cc * 2 * S + S + sp + 2, :]
                              .unsqueeze(0).broadcast_to([128, 2, T]),
                              waits=([("cC", vcoll)] if cc == 0 else []) +
                                    [("cV", rC_used[sp]),
                                     ("cV", rC_used[sp + 1])],
                              track=f"rC{(sp // 2) % 4}")
                    vC[sp] = vC[sp + 1] = v
                for j in range(4):
                    if j == 0:
                        emit_dt(0)
                        emit_dt(1)
                    jsl = j % 2
                    grp = 4 * jsl
                    vts_mine = vts_j[jsl]
                    dtmm_mine = dtmm_j[jsl]
                    vln_mine = vln_j[jsl]
                    sl = (cc * 4 + j) % 2
                    vxc = P.dma("sync", wct[sl][:, 0:T],
                                myag[cc * 512 + j * 128:cc * 512 + (j + 1) * 128, :],
                                waits=[vgat, ("cV", wct_used[sl])],
                                track=f"xcl{sl}")
                    vw = P.step("vector", lambda e, jsl=jsl, sl=sl: e.tensor_tensor(
                        out=wct[sl][:, 0:T], in0=dtt[jsl][:, 0:T],
                        in1=wct[sl][:, 0:T], op=OP.mult),
                        waits=[vxc, ("cA", vln_mine), ("cV", wct_used[sl])],
                        inc="cV")
                    vb = vw
                    for s in range(S):
                        hp = s % 2
                        col_sj = s * 4 + j
                        va = P.step("scalar", lambda e, jsl=jsl, j=j, s=s, hp=hp:
                                    e.activation(
                                        out=dAb[hp][:, 0:T],
                                        in_=dtt[jsl][:, 0:T],
                                        func=AF.Exp,
                                        scale=s_acol[:, j * S + s:j * S + s + 1]),
                                    waits=[("cA", vln_mine),
                                           ("cV", dA_slot[hp])],
                                    inc="cA")
                        vb = P.step("vector", lambda e, sl=sl, s=s:
                                    e.tensor_tensor(out=bbf[:, 0:T],
                                                    in0=wct[sl][:, 0:T],
                                                    in1=repB[s][:, 0:T],
                                                    op=OP.mult),
                                    waits=[("cV", vw), vB[s], ("cV", tts_prev)],
                                    inc="cV")
                        tts = P.step(
                            "vector", lambda e, hp=hp, col_sj=col_sj:
                            e.tensor_tensor_scan(
                                out=hhb[hp][:, 0:T], data0=dAb[hp][:, 0:T],
                                data1=bbf[:, 0:T],
                                initial=s_hst[:, col_sj:col_sj + 1],
                                op0=OP.mult, op1=OP.add),
                            waits=[("cA", va), ("cV", vb),
                                   ("cA", hcopy[col_sj]), ("cA", h_used[hp])],
                            inc="cV")
                        dA_slot[hp] = tts
                        tts_prev = tts
                        hcopy[col_sj] = P.step(
                            "scalar", lambda e, hp=hp, col_sj=col_sj: e.activation(
                                out=s_hst[:, col_sj:col_sj + 1],
                                in_=hhb[hp][:, T - 1:T], func=AF.Copy),
                            waits=[("cV", tts)], inc="cA")
                        h_used[hp] = hcopy[col_sj]
                        vgc = P.step("vector", lambda e, hp=hp, s=s:
                                     e.tensor_tensor(out=gcb[:, 0:T],
                                                     in0=hhb[hp][:, 0:T],
                                                     in1=repC[s][:, 0:T],
                                                     op=OP.mult),
                                     waits=[vC[s], ("cV", tts),
                                            ("cT", gc_mm_last)], inc="cV")
                        for q in range(4):
                            w = []
                            if q == 0:
                                w = [("cV", vgc), ("d0g", 32)]
                                if s == 0:
                                    w += [("cA", evac_tok[jsl]),
                                          ("cV", vts_mine),
                                          ("cT", dtmm_mine)]
                            gc_mm_last = P.step(
                                "tensor", lambda e, q=q, grp=grp, s=s: e.matmul(
                                    pb[grp + q][:, 0:512], s_identb[:],
                                    gcb[:, q * 512:(q + 1) * 512],
                                    start=(s == 0), stop=(s == S - 1)),
                                waits=w, inc="cT" if q == 3 else None)
                        rB_used[s] = vb
                        rC_used[s] = vgc
                    wct_used[sl] = vb
                    dtt_guard[jsl] = P.now("cA")
                    vev = 0
                    for q in range(4):
                        vev = P.step("scalar", lambda e, q=q, grp=grp:
                                     e.activation(
                                         out=ybo[:, q * 512:(q + 1) * 512],
                                         in_=pb[grp + q][:, 0:512],
                                         func=AF.Copy),
                                     waits=([("cT", gc_mm_last), ybo_guard]
                                            if q == 0 else []),
                                     inc="cA")
                    evac_tok[jsl] = vev
                    ybo_guard = P.dma(
                        "sync", ygi[cc * 512 + j * 128:cc * 512 + (j + 1) * 128, :],
                        ybo[:, 0:T], waits=[("cA", vev)], track="ygs")
                    if j + 2 < 4:
                        emit_dt(j + 2)
                P.step("gpsimd", lambda e, cc=cc: e.collective_compute(
                    "AllGather", OP.bypass, replica_groups=RG,
                    ins=[ygi[cc * 512:(cc + 1) * 512]],
                    outs=[ygo[cc * NC * 512:(cc + 1) * NC * 512]]),
                    waits=[("ygs", P.now("ygs"))], inc="cC")
            vcoll2 = P.now("cC")
            P.fence()

            if upto <= 4:
                stop_dump(ygi[0:128, :], ygi[512:640, :])
            SA.reset()
            _ysb = SA.bf16(4 * 2048)
            ysblk = [_ysb, _ysb]
            _xct = SA.f32(T)
            xct = [_xct, _xct]
            _szt = SA.f32(T)
            szt = [_szt, _szt]
            yab = SA.f32(T)
            ytile = [SA.bf16(2048), SA.bf16(2048)]
            yhalf = SA.bf16(NT_E * 1024)
            oslab = [SA.bf16(4096), SA.bf16(4096)]
            xfc = [SA.f32(512), SA.f32(512)]
            u0b = SA.f32(512)
            tcb = [SA.f32(512), SA.f32(512)]
            repm0 = SA.f32(T)
            repi0 = SA.f32(T)

            P.dma_dyn("sync", bass.AP(myy.tensor, 0, [[T, E], [1, T]]),
                      lambda: bass.AP(ygo.tensor, hold["off"] * (8 * T),
                                      [[T, E], [1, T]]),
                      waits=[("cC", vcoll2)], track="dmyy")
            vygat = ("dmyy", P.now("dmyy"))

            yprev = [0, 0]
            yt_used = [("ysp0", 0), ("ysp1", 0)]
            blk_used = [0, 0]
            for b in range(NC):
                vblk = P.dma(
                    "sync", ysblk[b % 2].rearrange("p (j t) -> p j t", j=4),
                    myy[b * 512:(b + 1) * 512, :].rearrange("(j p) t -> p j t", p=128),
                    waits=[vygat, ("cV", blk_used[b % 2])], track="ysb")
                for jj in range(4):
                    je = b * 4 + jj
                    yst_ap = ysblk[b % 2][:, jj * T:jj * T + T]
                    v2_ = P.dma("sync", xct[je % 2][:, 0:T],
                                xcf_sp[je * 128:(je + 1) * 128, :],
                                waits=[("cV", yprev[je % 2])],
                                track=f"gxc{je % 2}")
                    v3_ = P.dma("sync", szt[je % 2][:, 0:T],
                                sz_sp[je * 128:(je + 1) * 128, :],
                                waits=[("cV", yprev[je % 2])],
                                track=f"gsz{je % 2}")
                    vya = P.step("vector", lambda e, je=je, yst_ap=yst_ap:
                                 e.scalar_tensor_tensor(
                        out=yab[:, 0:T], in0=xct[je % 2][:, 0:T],
                        scalar=col("dsk", je), in1=yst_ap,
                        op0=OP.mult, op1=OP.add),
                        waits=[v2_, v3_, vblk, ("cV", yprev[1 - je % 2])],
                        inc="cV")
                    yprev[je % 2] = P.step("vector", lambda e, je=je: e.tensor_tensor(
                        out=ytile[je % 2][:, 0:T], in0=yab[:, 0:T],
                        in1=szt[je % 2][:, 0:T], op=OP.mult),
                        waits=[yt_used[je % 2], ("cV", vya)], inc="cV")
                    yt_used[je % 2] = P.dma("sync", y_sp[je * 128:(je + 1) * 128, :],
                                            ytile[je % 2][:, 0:T],
                                            waits=[("cV", yprev[je % 2])],
                                            track=f"ysp{je % 2}")
                    if jj == 3:
                        blk_used[b % 2] = yprev[je % 2]
            vm0 = P.dma("sync", repm0[:, 0:T], st_dr[0:128, 3:3 + T],
                        waits=[("dst1", P.now("dst1"))], track="dst2")
            vi0 = P.dma("sync", repi0[:, 0:T], st_dr[128:256, 3:3 + T],
                        waits=[("dst1", P.now("dst1"))], track="dst2")
            vm0 = vi0 = ("dst2", P.now("dst2"))

            tspv = []
            oslab_used = [0, 0]
            xfc_used = [0, 0]
            u0b_used = 0
            tcb_used = [("tsp0", 0), ("tsp1", 0)]
            bank_ev = [0] * 8
            for hf in range(2):
                hb = hf * 1024
                vy = 0
                for k in range(NT_E):
                    vy = P.dma("sync", yhalf[:, k * 1024:(k + 1) * 1024],
                               y_sp[k * 128:(k + 1) * 128, hb:hb + 1024],
                               waits=([("cT", oslab_used[0]), ("cT", oslab_used[1]),
                                       ("ysp0", P.now("ysp0")),
                                       ("ysp1", P.now("ysp1"))]
                                      if k == 0 else []), track="yhf",
                               selfwait=False)
                vy = ("yhf", P.now("yhf"))
                for m in range(NT_D):
                    mb = m % 2
                    vsl = slab_load(oslab[mb], w_op, m,
                                    waits=[("cT", oslab_used[mb])],
                                    track=f"osl{mb}")
                    for n2 in range(2):
                        c0 = hb + n2 * 512
                        bk = (m * 2 + n2) % 2
                        for k in range(NT_E):
                            w = [vy] if k == 0 else []
                            if k == 0:
                                w += [vsl, ("cV", bank_ev[bk])]
                            P.step("tensor", lambda e, mb=mb, k=k, n2=n2, bk=bk:
                                   e.matmul(pb[bk][:, 0:512],
                                            oslab[mb][:, k * 128:(k + 1) * 128],
                                            yhalf[:, k * 1024 + n2 * 512:
                                                  k * 1024 + n2 * 512 + 512],
                                            start=(k == 0), stop=(k == NT_E - 1)),
                                   waits=w, inc="cT" if k == NT_E - 1 else None)
                        vt = P.now("cT")
                        if n2 == 1:
                            oslab_used[mb] = vt
                        vx = P.dma("sync", xfc[n2][:, 0:512],
                                   xfm[m * 128:(m + 1) * 128, 3 + c0:3 + c0 + 512],
                                   waits=[("cV", xfc_used[n2])],
                                   track=f"xfc{n2}")
                        vu = P.step(
                            "vector", lambda e, n2=n2, c0=c0: e.tensor_tensor(
                                out=u0b[:, 0:512], in0=xfc[n2][:, 0:512],
                                in1=repm0[:, c0:c0 + 512], op=OP.subtract),
                            waits=[vx, vm0, ("cV", u0b_used)], inc="cV")
                        xfc_used[n2] = vu
                        vu = P.step("vector", lambda e, c0=c0: e.tensor_tensor(
                            out=u0b[:, 0:512], in0=u0b[:, 0:512],
                            in1=repi0[:, c0:c0 + 512], op=OP.mult),
                            waits=[vi0, ("cV", vu)], inc="cV")
                        vv = P.step("vector", lambda e, n2=n2, bk=bk: e.tensor_tensor(
                            out=tcb[n2][:, 0:512], in0=u0b[:, 0:512],
                            in1=pb[bk][:, 0:512], op=OP.add),
                            waits=[("cT", vt), tcb_used[n2], ("cV", vu)],
                            inc="cV")
                        u0b_used = vv
                        bank_ev[bk] = vv
                        tcb_used[n2] = P.dma(
                            "sync", t_sp[m * 128:(m + 1) * 128, c0:c0 + 512],
                            tcb[n2][:, 0:512], waits=[("cV", vv)],
                            track=f"tsp{n2}")
                        tspv.append(tcb_used[n2])
            P.fence()

            if upto <= 5:
                stop_dump(t_sp[0:128, :], y_sp[0:128, :])
            SA.reset()
            tb = [SA.bf16(2048), SA.bf16(2048)]
            sqb2 = [SA.bf16(2048), SA.bf16(2048)]
            ln2o = SA.bf16(NT_D * 2048)

            def l2t(k):
                return ln2o[:, k * 2048:k * 2048 + T]

            def t_load(j, buf, w):
                return P.dma("gpsimd", buf[:, 0:T],
                             t_sp[j * 128:(j + 1) * 128, :],
                             waits=list(w) + [("tsp0", P.now("tsp0")),
                                              ("tsp1", P.now("tsp1"))],
                             track=f"tld{j % 2}")
            repm2 = SA.f32(T)
            repi2 = SA.f32(T)
            sum2 = SA.f32(T)
            sq2 = SA.f32(T)
            sdone = stats_streamed(t_load, sqb2, tb, sum2, sq2)
            vr2 = ln_math(sdone, sum2, sq2, repm2, repi2, T, D)
            gb2 = [SA.bf16(2048), SA.bf16(2048)]
            lv = [0, 0]
            vstats_mm = P.now("cT")
            for j in range(NT_D):
                v = P.dma("gpsimd", tb[j % 2][:, 0:T], t_sp[j * 128:(j + 1) * 128, :],
                          waits=[("cV", lv[j % 2])] +
                                ([("cT", vstats_mm)] if j < 2 else []),
                          track=f"tld{j % 2}")
                vv0 = P.step("vector", lambda e, j=j: e.tensor_tensor(
                    out=gb2[j % 2][:, 0:T], in0=tb[j % 2][:, 0:T],
                    in1=repm2[:, 0:T], op=OP.subtract),
                    waits=[v, ("cV", vr2)], inc="cV")
                vv0 = P.step("vector", lambda e, j=j: e.scalar_tensor_tensor(
                    out=l2t(j), in0=gb2[j % 2][:, 0:T], scalar=col("ln2w", j),
                    in1=repi2[:, 0:T], op0=OP.mult, op1=OP.mult),
                    waits=[("cV", vv0)], inc="cV")
                lv[j % 2] = P.step("vector", lambda e, j=j: e.tensor_scalar(
                    out=l2t(j), in0=l2t(j), scalar1=col("ln2b", j), scalar2=None,
                    op0=OP.add), waits=[("cV", vv0)], inc="cV")
            vln2 = P.now("cV")

            fslab = [SA.bf16(2048) for _ in range(4)]
            sgb = [SA.bf16(512), SA.bf16(512)]
            hgb = [SA.bf16(512), SA.bf16(512)]
            fs_used = [0, 0, 0, 0]
            hg_used = [("hgs0", 0), ("hgs1", 0)]
            sg_used = [0, 0]
            bank_ev = [0] * 8
            for jp in range(32):
                sa, sg_ = jp % 2, 2 + jp % 2
                va_ = slab_load(fslab[sa], w_fp, jp, waits=[("cT", fs_used[sa])],
                                track=f"fsl{sa}")
                vg_ = slab_load(fslab[sg_], w_fp, jp + 32,
                                waits=[("cT", fs_used[sg_])], track=f"fsl{sg_}")
                for ci, (c0, cw) in enumerate(T_CH):
                    bka, bkg = 0, 1
                    for k in range(NT_D):
                        w = [("cV", vln2)] if (jp == 0 and ci == 0 and k == 0) else []
                        if k == 0:
                            w += [va_, ("cV", bank_ev[bka])]
                        P.step("tensor", lambda e, sa=sa, k=k, c0=c0, cw=cw, bka=bka:
                               e.matmul(pb[bka][:, 0:cw],
                                        fslab[sa][:, k * 128:(k + 1) * 128],
                                        l2t(k)[:, c0:c0 + cw],
                                        start=(k == 0), stop=(k == NT_D - 1)),
                               waits=w, inc="cT" if k == NT_D - 1 else None)
                    vta = P.now("cT")
                    for k in range(NT_D):
                        w = []
                        if k == 0:
                            w = [vg_, ("cA", bank_ev[bkg])]
                        P.step("tensor", lambda e, sg_=sg_, k=k, c0=c0, cw=cw, bkg=bkg:
                               e.matmul(pb[bkg][:, 0:cw],
                                        fslab[sg_][:, k * 128:(k + 1) * 128],
                                        l2t(k)[:, c0:c0 + cw],
                                        start=(k == 0), stop=(k == NT_D - 1)),
                               waits=w, inc="cT" if k == NT_D - 1 else None)
                    vtg = P.now("cT")
                    if ci == 3:
                        fs_used[sa] = vta
                        fs_used[sg_] = vtg
                    vsg = P.step("scalar", lambda e, ci=ci, cw=cw: e.activation(
                        out=sgb[ci % 2][:, 0:cw], in_=pb[1][:, 0:cw], func=AF.Silu),
                        waits=[("cT", vtg), ("cV", sg_used[ci % 2])], inc="cA")
                    bank_ev[1] = vsg
                    vhg = P.step("vector", lambda e, ci=ci, cw=cw: e.tensor_tensor(
                        out=hgb[ci % 2][:, 0:cw], in0=pb[0][:, 0:cw],
                        in1=sgb[ci % 2][:, 0:cw], op=OP.mult),
                        waits=[("cA", vsg), hg_used[ci % 2]], inc="cV")
                    sg_used[ci % 2] = vhg
                    bank_ev[0] = vhg
                    hg_used[ci % 2] = P.dma(
                        "sync", hg_sp[jp * 128:(jp + 1) * 128, c0:c0 + cw],
                        hgb[ci % 2][:, 0:cw], waits=[("cV", vhg)],
                        track=f"hgs{ci % 2}")
            P.fence()

            if upto <= 6:
                stop_dump(hg_sp[0:128, :], hg_sp[128:256, :])
            SA.reset()
            hghalf = SA.bf16(NT_E * 1024)
            sum3 = SA.f32(T)
            sq3 = SA.f32(T)
            foslab = [SA.bf16(4096), SA.bf16(4096)]
            tcc = [SA.f32(512), SA.f32(512)]
            atf = SA.f32(512)
            atb = [SA.bf16(512), SA.bf16(512)]
            sqb3 = [SA.bf16(512), SA.bf16(512)]

            fo_used = [0, 0]
            at_used = [("abs0", 0), ("abs1", 0)]
            tcc_used = [0, 0]
            sq_guard = 0
            bank_ev = [0] * 8
            for hf in range(2):
                hb = hf * 1024
                vy = 0
                for k in range(NT_E):
                    vy = P.dma("sync", hghalf[:, k * 1024:(k + 1) * 1024],
                               hg_sp[k * 128:(k + 1) * 128, hb:hb + 1024],
                               waits=[("cT", fo_used[0]), ("cT", fo_used[1])]
                               if k == 0 else [])
                for m in range(NT_D):
                    mb = m % 2
                    vsl = slab_load(foslab[mb], w_fo, m, waits=[("cT", fo_used[mb])])
                    for n2 in range(2):
                        c0 = hb + n2 * 512
                        ci_g = hf * 2 + n2
                        bk = n2 % 2
                        for k in range(NT_E):
                            w = [vy] if k == 0 else []
                            if k == 0:
                                w += [vsl, ("cV", bank_ev[bk])]
                            P.step("tensor", lambda e, mb=mb, k=k, n2=n2, bk=bk:
                                   e.matmul(pb[bk][:, 0:512],
                                            foslab[mb][:, k * 128:(k + 1) * 128],
                                            hghalf[:, k * 1024 + n2 * 512:
                                                   k * 1024 + n2 * 512 + 512],
                                            start=(k == 0), stop=(k == NT_E - 1)),
                                   waits=w, inc="cT" if k == NT_E - 1 else None)
                        vt = P.now("cT")
                        if n2 == 1:
                            fo_used[mb] = vt
                        vtl = P.dma("sync", tcc[n2][:, 0:512],
                                    t_sp[m * 128:(m + 1) * 128, c0:c0 + 512],
                                    waits=[("cV", tcc_used[n2])],
                                    track=f"tcc{n2}")
                        vv = P.step("vector", lambda e, n2=n2, bk=bk: e.tensor_tensor(
                            out=atf[:, 0:512], in0=tcc[n2][:, 0:512],
                            in1=pb[bk][:, 0:512], op=OP.add),
                            waits=[("cT", vt), vtl], inc="cV")
                        bank_ev[bk] = vv
                        tcc_used[n2] = vv
                        va = P.step("scalar", lambda e, n2=n2: e.activation(
                            out=atb[n2][:, 0:512], in_=atf[:, 0:512], func=AF.Copy),
                            waits=[("cV", vv), at_used[n2]], inc="cA")
                        at_used[n2] = P.dma(
                            "sync", ab_sp[m * 128:(m + 1) * 128, c0:c0 + 512],
                            atb[n2][:, 0:512], waits=[("cA", va)],
                            track=f"abs{n2}")
                        P.step("tensor", lambda e, n2=n2, ci_g=ci_g, m=m: e.matmul(
                            pb[2 + ci_g][:, 0:512], s_onef[:], atb[n2][:, 0:512],
                            start=(m == 0), stop=(m == NT_D - 1)),
                            waits=[("cA", va)] +
                                  ([("cA", sq_guard)] if m == 0 else []),
                            inc="cT" if m == NT_D - 1 else None)
                        vq = P.step("scalar", lambda e, n2=n2: e.activation(
                            out=sqb3[n2][:, 0:512], in_=atb[n2][:, 0:512],
                            func=AF.Square), waits=[("cA", va)], inc="cA")
                        P.step("tensor", lambda e, n2=n2, m=m: e.matmul(
                            pb[6 + n2][:, 0:512], s_onef[:], sqb3[n2][:, 0:512],
                            start=(m == 0), stop=(m == NT_D - 1)),
                            waits=[("cA", vq)] +
                                  ([("cA", sq_guard)] if m == 0 else []),
                            inc="cT" if m == NT_D - 1 else None)
                vt = P.now("cT")
                for n2 in range(2):
                    ci_g = hf * 2 + n2
                    c0 = hb + n2 * 512
                    P.step("scalar", lambda e, ci_g=ci_g, c0=c0: e.activation(
                        out=sum3[:, c0:c0 + 512], in_=pb[2 + ci_g][:, 0:512],
                        func=AF.Copy), waits=[("cT", vt)], inc="cA")
                    P.step("scalar", lambda e, n2=n2, c0=c0: e.activation(
                        out=sq3[:, c0:c0 + 512], in_=pb[6 + n2][:, 0:512],
                        func=AF.Copy), inc="cA")
                sq_guard = P.now("cA")
            P.fence()
            sdone = P.now("cA")
            repm3 = SA.f32(T)
            repi3 = SA.f32(T)
            vr3 = ln_math(sdone, sum3, sq3, repm3, repi3, T, D)
            _abt = SA.bf16(2048)
            abt = [_abt, _abt]
            xft = [SA.f32(T), SA.f32(T)]
            lnb = SA.f32(T)
            x1c = [SA.f32(T), SA.f32(T)]
            x1b = [SA.bf16(2048), SA.bf16(2048)]
            sqb4 = [SA.bf16(512), SA.bf16(512)]
            x1_used = [0, 0]
            x1mm = [0] * NT_D
            sqlast = [0, 0]
            sqv = [0, 0]
            for j in range(NT_D):
                jb = j % 2
                v1_ = P.dma("sync", abt[jb][:, 0:T], ab_sp[j * 128:(j + 1) * 128, :],
                            waits=[("cA", x1_used[jb]),
                                   ("abs0", P.now("abs0")),
                                   ("abs1", P.now("abs1"))], track="abl")
                v2_ = P.dma("sync", xft[jb][:, 0:T],
                            xfm[j * 128:(j + 1) * 128, 3:3 + T],
                            waits=[("cA", x1_used[jb])], track=f"xft{jb}")
                vv = P.step("vector", lambda e, jb=jb: e.tensor_tensor(
                    out=lnb[:, 0:T], in0=abt[jb][:, 0:T], in1=repm3[:, 0:T],
                    op=OP.subtract), waits=[v1_, ("cV", vr3)], inc="cV")
                vv = P.step("vector", lambda e, jb=jb: e.tensor_tensor(
                    out=lnb[:, 0:T], in0=lnb[:, 0:T], in1=repi3[:, 0:T],
                    op=OP.mult), waits=[("cV", vv)], inc="cV")
                vv = P.step("vector", lambda e, jb=jb: e.tensor_tensor(
                    out=x1c[jb][:, 0:T], in0=xft[jb][:, 0:T], in1=lnb[:, 0:T],
                    op=OP.add),
                    waits=[v2_, ("cV", vv), ("cA", x1_used[jb]),
                           (f"x1s{jb}", P.now(f"x1s{jb}"))],
                    inc="cV")
                P.dma("sync", x1_sp[j * 128:(j + 1) * 128, :], x1c[jb][:, 0:T],
                      waits=[("cV", vv)], track=f"x1s{jb}")
                va = P.step("scalar", lambda e, jb=jb: e.activation(
                    out=x1b[jb][:, 0:T], in_=x1c[jb][:, 0:T], func=AF.Copy),
                    waits=[("cV", vv)] + ([("cT", x1mm[j - 2])] if j >= 2 else []),
                    inc="cA")
                x1_used[jb] = va
                for ci, (c0, cw) in enumerate(T_CH):
                    P.step("tensor", lambda e, jb=jb, c0=c0, cw=cw, ci=ci, j=j:
                           e.matmul(pb[ci][:, 0:cw], s_onef[:],
                                    x1b[jb][:, c0:c0 + cw],
                                    start=(j == 0), stop=(j == NT_D - 1)),
                           waits=[("cA", va)] if ci == 0 else [],
                           inc=None)
                    sl = ci % 2
                    vq = P.step("scalar", lambda e, jb=jb, sl=sl, c0=c0, cw=cw:
                        e.activation(
                        out=sqb4[sl][:, 0:cw], in_=x1b[jb][:, c0:c0 + cw],
                        func=AF.Square),
                        waits=[("cA", va), ("cA", sqv[sl]), ("cT", sqlast[sl])],
                        inc="cA")
                    sqv[sl] = vq
                    sqlast[sl] = P.step(
                        "tensor", lambda e, sl=sl, cw=cw, ci=ci, j=j: e.matmul(
                            pb[4 + ci][:, 0:cw], s_onef[:], sqb4[sl][:, 0:cw],
                            start=(j == 0), stop=(j == NT_D - 1)),
                        waits=[("cA", vq)], inc="cT")
                    x1mm[j] = sqlast[sl]
            vt = P.now("cT")
            for ci, (c0, cw) in enumerate(T_CH):
                P.step("scalar", lambda e, c0=c0, cw=cw, ci=ci: e.activation(
                    out=sum3[:, c0:c0 + cw], in_=pb[ci][:, 0:cw], func=AF.Copy),
                    waits=[("cT", vt)] if ci == 0 else [], inc="cA")
                P.step("scalar", lambda e, c0=c0, cw=cw, ci=ci: e.activation(
                    out=sq3[:, c0:c0 + cw], in_=pb[4 + ci][:, 0:cw], func=AF.Copy),
                    inc="cA")
            sdone = P.now("cA")
            x1sum_sp = P.dma("sync", st_dr[0:128, 0:T], sum3[:, 0:T],
                             waits=[("cA", sdone)])
            x1sq_sp = P.dma("sync", st_dr[128:256, 0:T], sq3[:, 0:T],
                            waits=[("cA", sdone)])
            P.fence()

            if upto <= 7:
                stop_dump(x1_sp[0:128, :], ab_sp[0:128, :])
            SA.reset()
            repm4 = SA.f32(T)
            repi4 = SA.f32(T)
            sum4 = SA.f32(T)
            sq4 = SA.f32(T)
            vld1 = P.dma("sync", sum4[:, 0:T], st_dr[0:128, 0:T],
                         waits=[("dst3", P.now("dst3"))], track="dst4")
            vld2 = P.dma("sync", sq4[:, 0:T], st_dr[128:256, 0:T],
                         waits=[("dst3", P.now("dst3"))], track="dst4")
            _v4 = P.now("dst4")
            P.q["scalar"].append(lambda e: e.wait_ge(P.sems["dst4"], _v4))
            P.q["vector"].append(lambda e: e.wait_ge(P.sems["dst4"], _v4))
            vr4 = ln_math(P.now("cA"), sum4, sq4, repm4, repi4, T, D)
            x1t = [SA.bf16(2048), SA.bf16(2048)]
            gb4 = SA.bf16(2048)
            u1 = SA.bf16(NT_D * 2048)

            def u1t(k):
                return u1[:, k * 2048:k * 2048 + T]

            lv = [0, 0]
            for j in range(NT_D):
                jb = j % 2
                v = P.dma("gpsimd", x1t[jb][:, 0:T], x1_sp[j * 128:(j + 1) * 128, :],
                          waits=[("cV", lv[jb]), ("x1s0", P.now("x1s0")),
                                 ("x1s1", P.now("x1s1"))], track=f"x1l{jb}")
                vv0 = P.step("vector", lambda e, jb=jb: e.tensor_tensor(
                    out=gb4[:, 0:T], in0=x1t[jb][:, 0:T], in1=repm4[:, 0:T],
                    op=OP.subtract),
                    waits=[v, ("cV", vr4), ("cV", lv[1 - jb])], inc="cV")
                lv[jb] = P.step("vector", lambda e, j=j, jb=jb: e.tensor_tensor(
                    out=u1t(j), in0=gb4[:, 0:T], in1=repi4[:, 0:T], op=OP.mult),
                    waits=[("cV", vv0)], inc="cV")
            vu1 = P.now("cV")
            repal = SA.f32(D)
            val = P.dma("gpsimd", repal[:, 0:D],
                        al_dr.rearrange("a b -> (a b)").unsqueeze(0)
                        .broadcast_to([128, D]), track="dra")
            fnslab = [SA.bf16(2048), SA.bf16(2048)]
            sgn = SA.bf16(2048)
            bwslab = [SA.bf16(2048), SA.bf16(2048)]
            ob = [SA.bf16(512), SA.bf16(512)]
            fn_used = [0, 0]
            last_bw = 0
            ob_used = [("obs0", 0), ("obs1", 0)]
            bank_ev = [0] * 8
            for m in range(NT_D):
                mb = m % 2
                vsl = slab_load(fnslab[mb], w_fnt, m,
                                waits=[("cV", fn_used[mb])], track=f"fnl{mb}")
                va = P.step("scalar", lambda e, mb=mb: e.activation(
                    out=sgn[:, 0:2048], in_=fnslab[mb][:, 0:2048], func=AF.Sign),
                    waits=[vsl, ("cV", last_bw)], inc="cA")
                last_bw = P.step("vector", lambda e, mb=mb, m=m: e.tensor_tensor(
                    out=bwslab[mb][:, 0:2048].rearrange("p (a b) -> p a b", b=128),
                    in0=sgn[:, 0:2048].rearrange("p (a b) -> p a b", b=128),
                    in1=repal[:, m * 128:(m + 1) * 128].unsqueeze(1)
                    .broadcast_to([128, NT_D, 128]),
                    op=OP.mult), waits=[("cA", va), val], inc="cV")
                fn_used[mb] = last_bw
                for ci, (c0, cw) in enumerate(T_CH):
                    bk = ci % 2
                    for k in range(NT_D):
                        w = [("cV", vu1)] if (m == 0 and ci == 0 and k == 0) else []
                        if k == 0:
                            w += [("cV", fn_used[mb]), ("cV", bank_ev[bk])]
                        P.step("tensor", lambda e, mb=mb, k=k, c0=c0, cw=cw, bk=bk:
                               e.matmul(pb[bk][:, 0:cw],
                                        bwslab[mb][:, k * 128:(k + 1) * 128],
                                        u1t(k)[:, c0:c0 + cw],
                                        start=(k == 0), stop=(k == NT_D - 1)),
                               waits=w, inc="cT" if k == NT_D - 1 else None)
                    vt = P.now("cT")
                    va = P.step("vector", lambda e, m=m, ci=ci, cw=cw, bk=bk:
                                e.tensor_scalar(out=ob[ci % 2][:, 0:cw],
                                                in0=pb[bk][:, 0:cw],
                                                scalar1=col("fnb", m), scalar2=None,
                                                op0=OP.add),
                                waits=[("cT", vt), ob_used[ci % 2]],
                                inc="cV")
                    bank_ev[bk] = va
                    ob_used[ci % 2] = P.dma(
                        "sync", o_sp[m * 128:(m + 1) * 128, c0:c0 + cw],
                        ob[ci % 2][:, 0:cw], waits=[("cV", va)])
            P.fence()

            if upto <= 8:
                stop_dump(o_sp[0:128, :], o_sp[128:256, :])
            SA.reset()
            obt = [SA.bf16(2048), SA.bf16(2048)]
            sqb5 = [SA.bf16(2048), SA.bf16(2048)]

            def o_load(j, buf, w):
                return P.dma("gpsimd", buf[:, 0:T], o_sp[j * 128:(j + 1) * 128, :],
                             waits=w, track=f"old{j % 2}")
            repm5 = SA.f32(T)
            repi5 = SA.f32(T)
            sum5 = SA.f32(T)
            sq5 = SA.f32(T)
            sdone = stats_streamed(o_load, sqb5, obt, sum5, sq5)
            vr5 = ln_math(sdone, sum5, sq5, repm5, repi5, T, D)
            x1f = [SA.f32(T), SA.f32(T)]
            lnf = SA.f32(T)
            outc = [SA.f32(T), SA.f32(T)]
            out_used = [("os0", 0), ("os1", 0)]
            vstats_mm = P.now("cT")
            lastadd = 0
            for j in range(NT_D):
                jb = j % 2
                v1_ = P.dma("gpsimd", obt[jb][:, 0:T], o_sp[j * 128:(j + 1) * 128, :],
                            waits=[out_used[jb]] +
                                  ([("cT", vstats_mm)] if j < 2 else []),
                            track=f"olb{jb}")
                v2_ = P.dma("sync", x1f[jb][:, 0:T], x1_sp[j * 128:(j + 1) * 128, :],
                            waits=[out_used[jb]], track=f"x1f{jb}")
                vv = P.step("vector", lambda e, jb=jb: e.tensor_tensor(
                    out=lnf[:, 0:T], in0=obt[jb][:, 0:T], in1=repm5[:, 0:T],
                    op=OP.subtract),
                    waits=[v1_, ("cV", vr5), ("cV", lastadd)], inc="cV")
                vv = P.step("vector", lambda e, jb=jb: e.tensor_tensor(
                    out=lnf[:, 0:T], in0=lnf[:, 0:T], in1=repi5[:, 0:T],
                    op=OP.mult), waits=[("cV", vv)], inc="cV")
                vv = P.step("vector", lambda e, jb=jb: e.tensor_tensor(
                    out=outc[jb][:, 0:T], in0=x1f[jb][:, 0:T], in1=lnf[:, 0:T],
                    op=OP.add), waits=[v2_, ("cV", vv)], inc="cV")
                lastadd = vv
                out_used[jb] = P.dma("sync", out[j * 128:(j + 1) * 128, :],
                                     outc[jb][:, 0:T], waits=[("cV", vv)],
                                     track=f"os{jb}")
            P.fence()


        except _Stop:
            pass
        block = st.enter_context(nc.Block())
        P.emit(block)
    return nc


def _slabs(w):
    K, M = w.shape
    return np.ascontiguousarray(
        w.reshape(K, M // 128, 128).transpose(1, 0, 2))


def _cols(v):
    n = v.shape[0]
    return np.ascontiguousarray(v.reshape(n // 128, 128).T)


def stage(inputs):
    f32 = np.float32
    x = np.asarray(inputs["L0_attn_x"], f32)[0]
    ipw = np.asarray(inputs["L0_attn_in_proj_w"], f32)
    cvw = np.asarray(inputs["L0_attn_conv_w"], f32).reshape(E, 4)
    cvb = np.asarray(inputs["L0_attn_conv_b"], f32)
    xpw = np.asarray(inputs["L0_attn_x_proj_w"], f32)
    dtw = np.asarray(inputs["L0_attn_dt_proj_w"], f32)
    dtb = np.asarray(inputs["L0_attn_dt_proj_b"], f32)
    alog = np.asarray(inputs["L0_attn_A_log"], f32)
    dsk = np.asarray(inputs["L0_attn_D_skip"], f32)
    opw = np.asarray(inputs["L0_attn_out_proj_w"], f32)
    fpw = np.asarray(inputs["L0_attn_ff_proj_w"], f32)
    fow = np.asarray(inputs["L0_attn_ff_out_w"], f32)
    fnw = np.asarray(inputs["L0_ffn_weight"], f32)
    fnb = np.asarray(inputs["L0_ffn_bias"], f32)

    com = dict(
        w_ip=_slabs(np.ascontiguousarray(ipw.T)),
        w_xp=_slabs(np.concatenate(
            [np.ascontiguousarray(xpw.T),
             np.zeros((E, 96), f32)], axis=1)),

        w_op=_slabs(np.ascontiguousarray(opw.T)),
        w_fp=_slabs(np.ascontiguousarray(fpw.T)),
        w_fo=_slabs(np.ascontiguousarray(fow.T)),
        w_fnt=_slabs(np.ascontiguousarray(fnw.T)),
        w_fnr=fnw,
        cvw=cvw, cvb=_cols(cvb), dsk=_cols(dsk),
        ln1w=_cols(np.asarray(inputs["L0_attn_ln1_w"], f32)),
        ln1b=_cols(np.asarray(inputs["L0_attn_ln1_b"], f32)),
        ln2w=_cols(np.asarray(inputs["L0_attn_ln2_w"], f32)),
        ln2b=_cols(np.asarray(inputs["L0_attn_ln2_b"], f32)),
        fnb=_cols(fnb),
        ident=np.eye(128, dtype=f32),
        identu=np.eye(128, dtype=np.uint8),
        onecol=np.ones((128, 1), ml_dtypes.bfloat16),
        onefull=np.ones((128, 128), ml_dtypes.bfloat16),
        zcol=np.zeros((128, 64), f32),
        zb=np.zeros((128, 128), ml_dtypes.bfloat16),
        cvals=np.broadcast_to(
            np.array([1e-5, 1.0, 0.0, 0.0], f32), (128, 4)).copy(),
    )
    in_maps = []
    for c in range(NC):
        lo = c * T
        w = np.zeros((TH, D), f32)
        s0 = max(0, lo - 3)
        w[3 - (lo - s0):] = x[s0:lo + T]
        asl = alog[c * 512:(c + 1) * 512].reshape(4, 128, S)
        m = dict(com)
        m["xfm"] = np.ascontiguousarray(w.T)
        m["w_dts"] = np.ascontiguousarray(dtw[c * 512:(c + 1) * 512].T)
        m["dtbs"] = np.ascontiguousarray(
            dtb[c * 512:(c + 1) * 512].reshape(4, 128).T)
        m["alog"] = np.ascontiguousarray(
            asl.transpose(1, 0, 2).reshape(128, 4 * S))
        m["m3"] = np.full((128, 1), 0.0 if c == 0 else 1.0, f32)
        m["coff"] = np.array([[512 * c]], np.int32)
        m["coff2"] = np.array(
            [[(c // 2) * 8192 + (512 * c) % 1024]], np.int32)
        in_maps.append(m)
    return in_maps


def kernel(**inputs):
    nc = build(upto=9)
    in_maps = stage(inputs)
    res = run_bass_kernel_spmd(nc, in_maps, core_ids=list(range(NC)))
    outs = [np.asarray(res.results[c]["out"]).T for c in range(NC)]
    return np.concatenate(outs, axis=0).reshape(1, L, D).astype(np.float32)
